# version 01 (final)
# baseline (speedup 1.0000x reference)
from contextlib import ExitStack
from functools import reduce as _reduce

import numpy as np
import ml_dtypes
import concourse.bass as bass
import concourse.mybir as mybir
from concourse.bass_utils import run_bass_kernel_spmd

F32 = mybir.dt.float32
BF16 = mybir.dt.bfloat16
AF = mybir.ActivationFunctionType
OP = mybir.AluOpType
AX = mybir.AxisListType

B = 4
TG = 8192
T = 4096
D = 1024
DI = 2048
NST = 16
RNK = 64
H = 16
DH = 64
WIN = 256
DEPTH = 6
DFI = 2730
DFP = 2816
QK_SCALE = 8.0
EPS = 1e-5

TC = 512
NCH = T // TC
DC = DI // 128
CC = D // 128

FT = 512
NFT = T // FT
HALO = 264
NEG = -1.0e30

STAGE_LAYERS = 6
DEV_OUT = "x1d"
DEV_DEBUG = False
DEV_NCH = NCH
DEV_PHASES = (1, 2)

ENGS = ("sync", "tensor", "vector", "scalar", "gpsimd")


class Prog:
    def __init__(self, nc, ctx):
        self.nc = nc
        self.ctx = ctx
        self.q = {k: [] for k in ENGS}
        self.sems = {}
        self.cnt = {}

    def sem(self, name):
        h = self.ctx.enter_context(self.nc.semaphore(name))
        self.sems[name] = h
        self.cnt[name] = 0
        return h

    def emit(self, eng, make):
        self.q[eng].append(lambda e, mk=make: mk(e))

    def inc(self, eng, make, sem, n=1):
        s = self.sems[sem]
        def fn(e, mk=make, s=s, n=n):
            mk(e).then_inc(s, n)
        self.q[eng].append(fn)
        self.cnt[sem] += n
        return self.cnt[sem]

    def wait(self, eng, sem, val=None):
        v = self.cnt[sem] if val is None else val
        assert v <= self.cnt[sem], f"forward wait {sem}: {v} > {self.cnt[sem]}"
        s = self.sems[sem]
        def fn(e, s=s, v=v):
            e.wait_ge(s, v)
        self.q[eng].append(fn)

    def barrier(self, engines=ENGS):
        s = self.sems["s_bar"]
        for eng in engines:
            def fn(e, s=s):
                if hasattr(e, "engine_nop"):
                    e.engine_nop().then_inc(s, 1)
                else:
                    e.nop().then_inc(s, 1)
            self.q[eng].append(fn)
            self.cnt["s_bar"] += 1
        tgt = self.cnt["s_bar"]
        for eng in engines:
            self.wait(eng, "s_bar", tgt)


class Regions:

    def __init__(self, nc):
        self.nc = nc
        self.regions = {}
        self.order = []

    def region(self, name, size):
        self.regions[name] = [size, []]
        self.order.append(name)

    def place(self, region, name, shape, dtype, off=0):
        per = _reduce(lambda a, b: a * b, shape[1:], 1) * mybir.dt.size(dtype)
        r = self.regions[region]
        assert off + per <= r[0], \
            f"{name} ({off}+{per}) exceeds region {region} ({r[0]})"
        h = self.nc.virtual_sbuf_tensor(name, list(shape), dtype)
        r[1].append((h, off))
        return h

    def allocate(self, ctx):
        align = 64
        total = sum((r[0] + align - 1) // align * align
                    for r in self.regions.values())
        arena = ctx.enter_context(self.nc.sbuf_tensor(
            "arena", [128, total], mybir.dt.uint8))
        base = self.nc.lookup_mloc(arena).addr
        off = 0
        for name in self.order:
            size, items = self.regions[name]
            for h, toff in items:
                self.nc._tensor(h.name, h.shape, h.dtype, type="SB")
                mloc = self.nc.lookup_mloc(h)
                mloc.allocated = not self.nc.target_bir_lowering
                mloc.addr = base + off + toff
            off += (size + align - 1) // align * align


def nsl(w, cap=512):
    out = []
    o = 0
    while o < w:
        s = min(cap, w - o)
        out.append((o, s))
        o += s
    return out


KB = 1024


def build_program():
    nc = bass.Bass("TRN2", target_bir_lowering=False, debug=False)
    ctx = ExitStack()

    d = {}
    def din(name, shape, dt=F32):
        d[name] = nc.dram_tensor(name, list(shape), dt, kind="ExternalInput")
        return d[name]

    def dint(name, shape, dt=F32):
        d[name] = nc.dram_tensor(name, list(shape), dt, kind="Internal")
        return d[name]

    din("xs1", (D, 512 + T))
    din("xs2", (D, 512 + T))
    din("xnat", (D, T))
    for p in (1, 2):
        din(f"WinT{p}", (D, 2 * DI))
        din(f"Wconvd{p}", (128, DC * 4 * 128))
        din(f"WxT{p}", (DI, RNK + 2 * NST))
        din(f"WdtT{p}", (RNK, DI))
        din(f"WoutT{p}", (DI, D))
        din(f"Alog{p}", (128, DC * NST))
        din(f"bconv{p}", (128, DC))
        din(f"bdt{p}", (128, DC))
        din(f"Dp{p}", (128, DC))
    din("ln_w", (128, CC))
    din("ln_b", (128, CC))
    din("ones1", (1, 128))
    din("ones_c", (128, 1))
    din("cvals", (128, 16))
    din("onesb", (1, 128), BF16)
    for nm in ("m_id1", "m_rev1", "m_id2", "m_rev2", "m_lo", "m_hi"):
        din(nm, (128, 1))
    if STAGE_LAYERS >= 0:
        for li in range(DEPTH):
            din(f"WqkT{li}", (D, 2 * D))
            din(f"Wv{li}", (D, D))
            din(f"Wo{li}", (D, D))
            din(f"lng{li}", (128, CC))
            din(f"lnbf{li}", (128, CC))
            din(f"W1T{li}", (D, 2 * DFP))
            din(f"b1p{li}", (128, 2 * DFP // 128))
            din(f"W2T{li}", (DFP, D))
            din(f"b2{li}", (128, CC))
        din("rcq", (128, 512 + T)); din("rsq", (128, 512 + T))
        din("rck", (128, 512 + T)); din("rsk", (128, 512 + T))
        din("maskS", (NFT * 4 * 128, 3 * 128))
        din("maskP", (NFT * 4 * 3 * 128, 128), BF16)
        din("qsp", (128, DEPTH)); din("ksp", (128, DEPTH))
        din("blk2", (128, 2)); din("blk2b", (2, 128))
        din("blkA", (1, 128)); din("blkB", (1, 128))
        din("negb", (1, 128), BF16)
        din("onescb", (128, 1), BF16)
        din("idf32", (128, 128))
        din("rotm", (128, 128))

    out_t = nc.dram_tensor("out", [D, T], F32, kind="ExternalOutput")
    d["out"] = out_t
    if DEV_DEBUG:
        for nm, shape, dt in (("dbg_xn", (128, CC, TC), BF16),
                              ("dbg_xp", (128, DC, 4 + TC), BF16),
                              ("dbg_xc", (128, DC, TC), BF16),
                              ("dbg_dtl", (RNK, TC), BF16),
                              ("dbg_bc", (128, 8, TC), F32),
                              ("dbg_ys", (128, DC, TC), F32),
                              ("dbg_gated", (128, DC, TC), BF16),
                              ("dbg_tp", (128, 4, TC), F32),
                              ("dbg_h4", (128, TC, 4), F32),
                              ("dbg_pxn", (128, CC, 4), BF16),
                              ("dbg_pxp", (128, DC, 4), BF16),
                              ("dbg_pxb", (128, DC, 4), BF16),
                              ("dbg_q", (128, CC, FT), BF16),
                              ("dbg_k", (128, 2, CC, FT), BF16),
                              ("dbg_v", (128, 2, 4, H * DH), BF16),
                              ("dbg_o", (128, CC, FT), F32),
                              ("dbg_zs", (128, CC, FT), BF16),
                              ("dbg_pt", (128, 384), BF16),
                              ("dbg_den", (1, 2 * FT), F32)):
            d[nm] = nc.dram_tensor(nm, list(shape), dt, kind="ExternalOutput")

    dint("y1d", (D, T))
    dint("y2d", (D, T))
    dint("x1d", (D, T))
    dint("za", (D, T))
    dint("zb", (D, T))
    dint("bcd", (2, 32, TC))
    dint("cc_ci", (128, DC * NST))
    dint("cc_co", (256, DC * NST))
    if STAGE_LAYERS >= 0:
        dint("mrow", (1, 4 * H * 128), BF16)
        dint("cc_hi", (D, HALO))
        dint("cc_ho", (2 * D, HALO))
        dint("cc_ti", (D, 1))
        dint("cc_to", (2 * D, 1))

    pr = Prog(nc, ctx)
    for s in ("s_bar", "s_ld", "s_gd", "s_te", "s_dv", "s_ac", "s_gp", "s_cc",
              "s_st"):
        pr.sem(s)

    R = Regions(nc)
    R.region("RA", 16 * KB)
    R.region("RB", 16 * KB)
    R.region("RC", 16 * KB)
    R.region("RE", 16 * KB)
    R.region("RF", 8 * KB)
    R.region("RG", 32 * KB)
    R.region("RH", 516 * 2 * DC)
    R.region("RH2", 516 * 2 * DC)
    R.region("RI", 8 * KB)
    R.region("RJ", 16 * KB)
    R.region("RL", 8 * KB)
    R.region("RM", 2 * KB)
    R.region("RN", 25 * KB)

    sb = {}
    def place(reg, name, shape, dt=F32, off=0):
        sb[name] = R.place(reg, name, shape, dt, off=off)
        return sb[name]

    place("RA", "xin", (128, CC, TC))
    place("RB", "scr8", (128, CC, TC))
    place("RB", "gated", (128, DC, TC), BF16)
    place("RC", "xc_bf", (128, DC, TC), BF16)
    place("RC", "yout", (128, CC, TC))
    place("RE", "bc4", (128, 8, TC))
    place("RE", "yrev", (128, CC, TC))
    place("RF", "h4", (128, TC, 4))
    place("RG", "y_s", (128, DC, TC))
    place("RG", "rows", (1, 5 * TC))
    place("RH", "xp_a", (128, DC, 4 + TC), BF16)
    place("RH2", "xp_b", (128, DC, 4 + TC), BF16)
    place("RI", "xn_bf", (128, CC, TC), BF16)
    place("RJ", "wout_t", (128, 2, DC, 256), BF16)
    place("RJ", "win_t", (128, 2, CC, 256), BF16)
    place("RA", "ebt", (128, 4, TC))
    place("RL", "tp", (128, 4, TC))
    place("RM", "carrysel", (128, 2 * DC * NST))
    if STAGE_LAYERS >= 0:
        place("RA", "kbuf", (128, 2, CC, FT), BF16)
        place("RB", "o32", (128, CC, FT))
        place("RC", "scratch32", (128, CC, FT))
        place("RE", "wv_sb", (128, CC, D), BF16)
        place("RF", "mskS_sb", (128, 4, 384))
        place("RG", "mrowr", (1, 4 * H * 128), BF16)
        place("RG", "ag_bf", (128, DFP // 128, FT), BF16)
        place("RH", "zt", (128, CC, 516))
        place("RH2", "zstage", (128, CC, 516))
        place("RI", "zs_bf", (128, CC, FT), BF16)
        place("RI", "znf_bf", (128, CC, FT), BF16)
        place("RJ", "wqk_t", (128, 2, CC, 256), BF16)
        place("RJ", "wo_t", (128, 2, CC, 128), BF16, off=8192)
        place("RJ", "w1_t", (128, 2, CC, 256), BF16)
        place("RJ", "w2_t", (128, DFP // 128, 128), BF16, off=8192)
        place("RL", "mskP_sb", (128, 12, 128), BF16)
        place("RL", "den", (1, 2 * FT), off=4096)
        place("RN", "vbuf", (128, 2, 4, H * DH), BF16)
        place("RN", "q_bf", (128, CC, FT), BF16, off=16384)

    def Sreal(name, shape, dt=F32):
        sb[name] = ctx.enter_context(nc.sbuf_tensor(name, list(shape), dt))
        return sb[name]

    place("RN", "convd_t", (128, 2, 4, 128), BF16, off=0)
    place("RN", "wx_t", (128, DC, RNK + 2 * NST), BF16, off=2048)
    place("RN", "wdt_t", (RNK, DI), BF16, off=5120)
    place("RN", "winz_t", (128, 2, CC, 128), BF16, off=9216)
    place("RN", "dtlin", (RNK, TC), BF16, off=13312)
    place("RN", "browB", (1, TC), off=14336)
    place("RN", "browC", (1, TC), off=16384)
    place("RN", "Adev", (128, DC * NST), off=18432)
    Sreal("carry", (128, DC * NST))
    Sreal("bconv_t", (128, DC))
    Sreal("bdt_t", (128, DC))
    Sreal("D_t", (128, DC))
    Sreal("lnw_t", (128, CC))
    Sreal("lnb_t", (128, CC))
    Sreal("ones1_t", (1, 128))
    Sreal("onesc_t", (128, 1))
    Sreal("cvals_t", (128, 16))
    Sreal("onesb_t", (1, 128), BF16)
    for nm in ("mid1", "mrev1", "mid2", "mrev2", "mlo", "mhi"):
        Sreal(nm, (128, 1))
    if STAGE_LAYERS >= 0:
        Sreal("tok_sb", (128, CC, 2))
        Sreal("m_cols", (128, 64))
        Sreal("mrow_sb", (64, 128), BF16)
        Sreal("pt_bf", (128, 384), BF16)
        Sreal("den2", (2, FT))
        Sreal("qsp_t", (128, DEPTH)); Sreal("ksp_t", (128, DEPTH))
        Sreal("blk2_t", (128, 2)); Sreal("blk2b_t", (2, 128))
        Sreal("blkA_t", (1, 128)); Sreal("blkB_t", (1, 128))
        Sreal("negb_t", (1, 128), BF16)
        Sreal("onescb_t", (128, 1), BF16)
        Sreal("idf32_t", (128, 128))
        Sreal("rotm_t", (128, 128))
        Sreal("lng_t", (128, CC)); Sreal("lnbf_t", (128, CC))
        Sreal("b1p_t", (128, 2 * DFP // 128)); Sreal("b2_t", (128, CC))

    R.allocate(ctx)

    CVALS = [EPS, 1.0, -1.0, QK_SCALE, 0.5, 1.0 / D, 0.0, 2.0]
    for i, v in enumerate(CVALS):
        nc.const_aps.register(F32, v, sb["cvals_t"][:, i:i + 1])

    ps = {}
    def PS(name, shape, dt=F32):
        ps[name] = ctx.enter_context(nc.psum_tensor(name, list(shape), dt))
        return ps[name]

    PS("pA", (128, 512))
    PS("pB", (128, 512))
    PS("pS", (128, 512))
    PS("pS2", (128, 512))
    PS("pQ", (128, 512))
    PS("pR", (128, 512))
    PS("pV", (128, 512))
    PS("pW", (128, 512))

    def dma(eng, sem, out, in_):
        return pr.inc(eng, lambda e, o=out, i=in_: e.dma_start(out=o, in_=i),
                      sem, 16)

    fm = lambda t: t.ap().rearrange("(c p) t -> p c t", p=128)

    xs_by_phase = {1: d["xs1"], 2: d["xs2"]}

    def emit_load_consts():
        dma("sync", "s_ld", sb["ones1_t"][:, :], d["ones1"][:, :])
        dma("sync", "s_ld", sb["onesc_t"][:, :], d["ones_c"][:, :])
        dma("sync", "s_ld", sb["cvals_t"][:, :], d["cvals"][:, :])
        dma("sync", "s_ld", sb["onesb_t"][:, :], d["onesb"][:, :])
        dma("sync", "s_ld", sb["lnw_t"][:, :], d["ln_w"][:, :])
        dma("sync", "s_ld", sb["lnb_t"][:, :], d["ln_b"][:, :])
        for nm, src in (("mid1", "m_id1"), ("mrev1", "m_rev1"),
                        ("mid2", "m_id2"), ("mrev2", "m_rev2"),
                        ("mlo", "m_lo"), ("mhi", "m_hi")):
            dma("sync", "s_ld", sb[nm][:, :], d[src][:, :])
        if STAGE_LAYERS >= 0:
            for nm in ("qsp", "ksp", "blk2", "blk2b", "blkA", "blkB", "negb",
                       "onescb", "idf32", "rotm"):
                dma("sync", "s_ld", sb[nm + "_t"][:, :], d[nm][:, :])
        pr.wait("sync", "s_ld")
        pr.barrier()

    def emit_load_phase_weights(p):
        dma("gpsimd", "s_gd", sb["wx_t"][:, :, :],
            d[f"WxT{p}"].ap().rearrange("(k p) m -> p k m", p=128))
        dma("gpsimd", "s_gd", sb["wdt_t"][:, :], d[f"WdtT{p}"][:, :])
        dma("sync", "s_ld", sb["bconv_t"][:, :], d[f"bconv{p}"][:, :])
        dma("sync", "s_ld", sb["bdt_t"][:, :], d[f"bdt{p}"][:, :])
        dma("sync", "s_ld", sb["D_t"][:, :], d[f"Dp{p}"][:, :])
        dma("sync", "s_ld", sb["Adev"][:, :], d[f"Alog{p}"][:, :])
        pr.wait("scalar", "s_ld")
        pr.wait("scalar", "s_gd")
        a1 = pr.inc("scalar", lambda e: e.activation(
            out=sb["Adev"][:, :], in_=sb["Adev"][:, :], func=AF.Exp), "s_ac")
        pr.wait("vector", "s_ac", a1)
        pr.inc("vector", lambda e: e.tensor_scalar_mul(
            out=sb["Adev"][:, :], in0=sb["Adev"][:, :], scalar1=-1.0), "s_dv")
        pr.barrier()

    def emit_ln(src_ap, w, dst_bf=None, keep_x=False):
        xin, scr, rows = sb["xin"], sb["scr8"], sb["rows"]
        Rr = lambda i: rows[0:1, i * TC: i * TC + w]
        ld = dma("sync", "s_ld", xin[:, :, :w], src_ap)
        pr.wait("scalar", "s_ld", ld)
        sq = pr.inc("scalar", lambda e: e.activation(
            out=scr[:, :, :w], in_=xin[:, :, :w], func=AF.Square), "s_ac")
        pr.wait("tensor", "s_ld", ld)
        pr.wait("tensor", "s_ac", sq)
        def mk_stats(e, w=w):
            last = None
            for src, pp in ((xin, ps["pS"]), (scr, ps["pS2"])):
                for c in range(CC):
                    last = e.matmul(out=pp[0:1, :w], lhsT=sb["onesc_t"][:, :],
                                    rhs=src[:, c, :w],
                                    start=(c == 0), stop=(c == CC - 1))
            return last
        st = pr.inc("tensor", mk_stats, "s_te")
        pr.wait("vector", "s_te", st)
        pr.inc("vector", lambda e: e.tensor_scalar_mul(
            out=Rr(0), in0=ps["pS"][0:1, :w], scalar1=1.0 / D), "s_dv")
        pr.inc("vector", lambda e: e.tensor_scalar_mul(
            out=Rr(1), in0=ps["pS2"][0:1, :w], scalar1=1.0 / D), "s_dv")
        pr.inc("vector", lambda e: e.tensor_mul(out=Rr(2), in0=Rr(0), in1=Rr(0)), "s_dv")
        dv = pr.inc("vector", lambda e: e.tensor_sub(out=Rr(1), in0=Rr(1), in1=Rr(2)), "s_dv")
        pr.wait("scalar", "s_dv", dv)
        ac = pr.inc("scalar", lambda e: e.activation(
            out=Rr(2), in_=Rr(1), func=AF.Sqrt, bias=EPS), "s_ac")
        pr.wait("vector", "s_ac", ac)
        dv = pr.inc("vector", lambda e: e.reciprocal(out=Rr(3), in_=Rr(2)), "s_dv")
        if dst_bf is not None:
            pr.inc("vector", lambda e: e.tensor_mul(out=Rr(4), in0=Rr(1), in1=Rr(3)), "s_dv")
            dv = pr.inc("vector", lambda e: e.tensor_mul(out=Rr(4), in0=Rr(4), in1=Rr(3)), "s_dv")
            pr.wait("scalar", "s_dv", dv)
            ac = pr.inc("scalar", lambda e: e.activation(
                out=Rr(2), in_=Rr(4), func=AF.Sqrt, bias=EPS), "s_ac")
            pr.wait("vector", "s_ac", ac)
            pr.inc("vector", lambda e: e.reciprocal(out=Rr(4), in_=Rr(2)), "s_dv")
            dv = pr.inc("vector", lambda e: e.tensor_mul(out=Rr(3), in0=Rr(3), in1=Rr(4)), "s_dv")
        pr.wait("tensor", "s_dv", dv)
        def mk_bc(e, w=w):
            e.matmul(out=ps["pS"][:, :w], lhsT=sb["ones1_t"][:, :], rhs=Rr(0),
                     start=True, stop=True)
            return e.matmul(out=ps["pS2"][:, :w], lhsT=sb["ones1_t"][:, :],
                            rhs=Rr(3), start=True, stop=True)
        bc = pr.inc("tensor", mk_bc, "s_te")
        pr.wait("vector", "s_te", bc)
        for c in range(CC):
            pr.inc("vector", lambda e, c=c: e.tensor_sub(
                out=scr[:, c, :w], in0=xin[:, c, :w], in1=ps["pS"][:, :w]), "s_dv")
            if dst_bf is not None:
                pr.inc("vector", lambda e, c=c: e.tensor_mul(
                    out=scr[:, c, :w], in0=scr[:, c, :w], in1=ps["pS2"][:, :w]),
                    "s_dv")
                dv = pr.inc("vector", lambda e, c=c: e.tensor_scalar(
                    out=dst_bf[:, c, :w], in0=scr[:, c, :w],
                    scalar1=sb["lnw_t"][:, c:c + 1],
                    scalar2=sb["lnb_t"][:, c:c + 1],
                    op0=OP.mult, op1=OP.add), "s_dv")
            else:
                dv = pr.inc("vector", lambda e, c=c: e.tensor_mul(
                    out=scr[:, c, :w], in0=scr[:, c, :w], in1=ps["pS2"][:, :w]),
                    "s_dv")
        return dv

    def emit_win_xp(p, w, dst_off):
        te_done = {}
        done = {}
        for mb in range(8):
            buf = mb % 2
            if mb >= 2:
                pr.wait("gpsimd", "s_te", te_done[mb * 2 - 3])
            gd = dma("gpsimd", "s_gd", sb["win_t"][:, buf, :, :],
                     d[f"WinT{p}"].ap().rearrange("(k p) m -> p k m", p=128)
                     [:, :, mb * 256:(mb + 1) * 256])
            pr.wait("tensor", "s_gd", gd)
            for m01 in (0, 1):
                dc = mb * 2 + m01
                pp = ps["pA"] if dc % 2 == 0 else ps["pB"]
                if dc >= 2:
                    pr.wait("tensor", done[dc - 2][0], done[dc - 2][1])
                def mk(e, buf=buf, m01=m01, pp=pp, w=w):
                    last = None
                    for k in range(CC):
                        last = e.matmul(
                            out=pp[:, :w],
                            lhsT=sb["win_t"][:, buf, k, m01 * 128:(m01 + 1) * 128],
                            rhs=sb["xn_bf"][:, k, :w],
                            start=(k == 0), stop=(k == CC - 1))
                    return last
                te_done[dc] = pr.inc("tensor", mk, "s_te")
                if dst_off is None:
                    pr.wait("scalar", "s_te", te_done[dc])
                    pr.inc("scalar", lambda e, dc=dc, pp=pp: e.copy(
                        out=sb["xp_a"][:, dc, 0:4], in_=pp[:, 508:512]), "s_ac")
                    ac = pr.inc("scalar", lambda e, dc=dc, pp=pp: e.copy(
                        out=sb["xp_b"][:, dc, 0:3], in_=pp[:, 509:512]), "s_ac")
                    done[dc] = ("s_ac", ac)
                else:
                    pr.wait("scalar", "s_te", te_done[dc])
                    ac = pr.inc("scalar", lambda e, dc=dc, pp=pp, w=w: e.copy(
                        out=sb["xp_a"][:, dc, 4:4 + w], in_=pp[:, :w]), "s_ac")
                    pr.wait("vector", "s_ac", ac)
                    pr.inc("vector", lambda e, dc=dc, w=w: e.tensor_copy(
                        out=sb["xp_b"][:, dc, 3:3 + w],
                        in_=sb["xp_a"][:, dc, 4:4 + w]), "s_dv")
                    done[dc] = ("s_ac", ac)
        pr.barrier(("tensor", "scalar", "gpsimd", "vector"))

    def emit_conv(p, dc, w, out_ap, pp):
        buf = dc % 2
        pr.wait("gpsimd", "s_te")
        gd = dma("gpsimd", "s_gd", sb["convd_t"][:, buf, :, :],
                 d[f"Wconvd{p}"].ap().rearrange("p (dc k m) -> p dc k m",
                                                dc=DC, k=4)[:, dc, :, :])
        pr.wait("tensor", "s_gd", gd)
        def mk(e, dc=dc, w=w, pp=pp, buf=buf):
            srcs = ((sb["xp_b"], 0), (sb["xp_a"], 2), (sb["xp_b"], 2),
                    (sb["xp_a"], 4))
            last = None
            for k in range(4):
                tt, o0 = srcs[k]
                last = e.matmul(out=pp[:, :w],
                                lhsT=sb["convd_t"][:, buf, k, :],
                                rhs=tt[:, dc, o0:o0 + w],
                                start=(k == 0), stop=(k == 3))
            return last
        te = pr.inc("tensor", mk, "s_te")
        pr.wait("scalar", "s_te", te)
        ac = pr.inc("scalar", lambda e, dc=dc, pp=pp, w=w: e.activation(
            out=out_ap, in_=pp[:, :w], func=AF.Silu,
            bias=sb["bconv_t"][:, dc:dc + 1]), "s_ac")
        return ac

    def emit_mamba_chunk(p, c, xs, ydst):
        w = TC
        off = 512 + c * TC
        if c > 0:
            pr.inc("vector", lambda e: e.tensor_copy(
                out=sb["xp_a"][:, :, 0:4], in_=sb["xp_a"][:, :, TC:TC + 4]),
                "s_dv")
            pr.inc("vector", lambda e: e.tensor_copy(
                out=sb["xp_b"][:, :, 0:3], in_=sb["xp_b"][:, :, TC:TC + 3]),
                "s_dv")
        pr.barrier()
        emit_ln(fm(xs)[:, :, off:off + w], w, dst_bf=sb["xn_bf"])
        pr.barrier()
        emit_win_xp(p, w, 4)
        for dc in range(DC):
            pp = ps["pA"] if dc % 2 == 0 else ps["pB"]
            ac = emit_conv(p, dc, w, sb["xc_bf"][:, dc, :w], pp)
            pr.wait("tensor", "s_ac", ac)
        pr.barrier(("tensor", "scalar", "gpsimd", "vector", "sync"))
        def mk_xdbl(e, w=w):
            last = None
            for k in range(DC):
                last = e.matmul(out=ps["pS"][0:96, :w],
                                lhsT=sb["wx_t"][:, k, :],
                                rhs=sb["xc_bf"][:, k, :w],
                                start=(k == 0), stop=(k == DC - 1))
            return last
        te = pr.inc("tensor", mk_xdbl, "s_te")
        pr.wait("scalar", "s_te", te)
        pr.inc("scalar", lambda e, w=w: e.copy(
            out=sb["dtlin"][:, :w], in_=ps["pS"][0:64, :w]), "s_ac")
        a2 = pr.inc("scalar", lambda e, w=w: e.copy(
            out=sb["tp"][0:32, 0, :w], in_=ps["pS"][64:96, :w]), "s_ac")
        pr.wait("sync", "s_ac", a2)
        bcw = dma("sync", "s_ld", d["bcd"].ap()[c % 2, :, :w], sb["tp"][0:32, 0, :w])
        pr.barrier()

        for nb in range(4):
            pr.wait("sync", "s_ld", bcw)
            for n8 in range(4):
                n = nb * 4 + n8
                l1 = dma("sync", "s_ld", sb["browB"][0:1, :w],
                         d["bcd"].ap()[c % 2, n:n + 1, :w])
                l2 = dma("sync", "s_ld", sb["browC"][0:1, :w],
                         d["bcd"].ap()[c % 2, NST + n:NST + n + 1, :w])
                pr.wait("tensor", "s_ld", l2)
                def mkb(e, w=w):
                    e.matmul(out=ps["pR"][:, :w], lhsT=sb["ones1_t"][:, :],
                             rhs=sb["browB"][0:1, :w], start=True, stop=True)
                    return e.matmul(out=ps["pQ"][:, :w], lhsT=sb["ones1_t"][:, :],
                                    rhs=sb["browC"][0:1, :w], start=True, stop=True)
                te = pr.inc("tensor", mkb, "s_te")
                pr.wait("scalar", "s_te", te)
                pr.inc("scalar", lambda e, n8=n8, w=w: e.copy(
                    out=sb["bc4"][:, n8, :w], in_=ps["pR"][:, :w]), "s_ac")
                ac = pr.inc("scalar", lambda e, n8=n8, w=w: e.copy(
                    out=sb["bc4"][:, 4 + n8, :w], in_=ps["pQ"][:, :w]), "s_ac")
                pr.wait("tensor", "s_ac", ac)
            pr.barrier(("tensor", "scalar", "vector", "gpsimd", "sync"))
            for dc in range(DC):
                emit_scan_dc(p, c, nb, dc)
            pr.barrier(("tensor", "scalar", "vector", "gpsimd"))
        if DEV_DEBUG and p == 1 and c == 0:
            for nm, t in (("dbg_xn", "xn_bf"), ("dbg_xp", "xp_a"),
                          ("dbg_xc", "xc_bf"), ("dbg_dtl", "dtlin"),
                          ("dbg_bc", "bc4"), ("dbg_ys", "y_s"),
                          ("dbg_gated", "gated"), ("dbg_tp", "tp"),
                          ("dbg_h4", "h4")):
                dma("sync", "s_st", d[nm].ap(), sb[t].ap())
            pr.wait("sync", "s_st")
            pr.barrier()
        emit_wout(p, c, ydst)
        pr.barrier()

    def emit_scan_dc(p, c, nb, dc):
        w = TC
        pr.wait("scalar", "s_dv")
        pr.wait("gpsimd", "s_dv")
        pr.wait("tensor", "s_dv")
        pr.wait("tensor", "s_ac")
        tp = sb["tp"]
        t_dt = tp[:, 0, :]
        t_a = tp[:, 1, :]
        t_b = tp[:, 2, :]
        t_c = tp[:, 3, :]
        def mkdt(e, dc=dc, w=w):
            return e.matmul(out=ps["pQ"][:, :w],
                            lhsT=sb["wdt_t"][:, dc * 128:(dc + 1) * 128],
                            rhs=sb["dtlin"][:, :w], start=True, stop=True)
        te = pr.inc("tensor", mkdt, "s_te")
        pr.wait("scalar", "s_te", te)
        pr.inc("scalar", lambda e, dc=dc: e.activation(
            out=t_a, in_=ps["pQ"][:, :w], func=AF.Relu,
            bias=sb["bdt_t"][:, dc:dc + 1]), "s_ac")
        ac = pr.inc("scalar", lambda e, dc=dc: e.activation(
            out=t_b, in_=ps["pQ"][:, :w], func=AF.Abs,
            bias=sb["bdt_t"][:, dc:dc + 1]), "s_ac")
        pr.wait("tensor", "s_ac", ac)
        pr.inc("scalar", lambda e: e.activation(
            out=t_b, in_=t_b, func=AF.Exp, scale=-1.0), "s_ac")
        ac = pr.inc("scalar", lambda e: e.activation(
            out=t_b, in_=t_b, func=AF.Ln, bias=1.0), "s_ac")
        pr.wait("vector", "s_ac", ac)
        dv_dt = pr.inc("vector", lambda e: e.tensor_add(
            out=t_dt, in0=t_a, in1=t_b), "s_dv")
        ac_xc = emit_conv(p, dc, w, t_a, ps["pQ"])
        pr.wait("vector", "s_ac", ac_xc)
        dv_w = pr.inc("vector", lambda e: e.tensor_mul(
            out=t_b, in0=t_dt, in1=t_a), "s_dv")
        pr.wait("gpsimd", "s_dv", dv_w)
        pr.wait("scalar", "s_dv", dv_dt)
        dvS = {}
        for n8 in range(4):
            n = nb * 4 + n8
            eb = n8 % 2
            if n8 >= 2:
                pr.wait("scalar", "s_dv", dvS[n8 - 2])
                pr.wait("gpsimd", "s_dv", dvS[n8 - 2])
            acE = pr.inc("scalar", lambda e, dc=dc, n=n, eb=eb: e.activation(
                out=sb["ebt"][:, eb, :], in_=t_dt, func=AF.Exp,
                scale=sb["Adev"][:, dc * NST + n: dc * NST + n + 1]), "s_ac")
            gpB = pr.inc("gpsimd", lambda e, n8=n8, eb=eb: e.tensor_mul(
                out=sb["ebt"][:, 2 + eb, :], in0=t_b,
                in1=sb["bc4"][:, n8, :]), "s_gp")
            pr.wait("vector", "s_ac", acE)
            pr.wait("vector", "s_gp", gpB)
            dvS[n8] = pr.inc("vector",
                             lambda e, dc=dc, n=n, n8=n8, eb=eb: e.tensor_tensor_scan(
                out=sb["h4"][:, :, n8], data0=sb["ebt"][:, eb, :],
                data1=sb["ebt"][:, 2 + eb, :],
                initial=sb["carry"][:, dc * NST + n: dc * NST + n + 1],
                op0=OP.mult, op1=OP.add), "s_dv")
        pr.inc("vector", lambda e, dc=dc, nb=nb: e.tensor_copy(
            out=sb["carry"][:, dc * NST + nb * 4: dc * NST + nb * 4 + 4],
            in_=sb["h4"][:, TC - 1, :]), "s_dv")
        for n8 in range(4):
            pr.inc("vector", lambda e, n8=n8: e.tensor_mul(
                out=sb["h4"][:, :, n8], in0=sb["h4"][:, :, n8],
                in1=sb["bc4"][:, 4 + n8, :]), "s_dv")
        if nb == 0:
            pr.inc("vector", lambda e, dc=dc: e.reduce_sum(
                out=sb["y_s"][:, dc, :], in_=sb["h4"][:, :, :], axis=AX.X), "s_dv")
        else:
            pr.inc("vector", lambda e: e.reduce_sum(
                out=t_c, in_=sb["h4"][:, :, :], axis=AX.X), "s_dv")
            dv_ya = pr.inc("vector", lambda e, dc=dc: e.tensor_add(
                out=sb["y_s"][:, dc, :], in0=sb["y_s"][:, dc, :], in1=t_c), "s_dv")
        if nb == 3:
            buf = dc % 2
            pr.wait("gpsimd", "s_te")
            gd = dma("gpsimd", "s_gd", sb["winz_t"][:, buf, :, :],
                     d[f"WinT{p}"].ap().rearrange("(k p) m -> p k m", p=128)
                     [:, :, DI + dc * 128: DI + (dc + 1) * 128])
            pr.wait("tensor", "s_gd", gd)
            def mkz(e, buf=buf, w=w):
                last = None
                for k in range(CC):
                    last = e.matmul(out=ps["pR"][:, :w],
                                    lhsT=sb["winz_t"][:, buf, k, :],
                                    rhs=sb["xn_bf"][:, k, :w],
                                    start=(k == 0), stop=(k == CC - 1))
                return last
            tez = pr.inc("tensor", mkz, "s_te")
            pr.wait("scalar", "s_te", tez)
            pr.wait("scalar", "s_dv", dv_ya)
            acz = pr.inc("scalar", lambda e, w=w: e.activation(
                out=t_c, in_=ps["pR"][:, :w], func=AF.Silu), "s_ac")
            pr.wait("vector", "s_ac", acz)
            pr.inc("vector", lambda e, dc=dc: e.scalar_tensor_tensor(
                out=t_b, in0=t_a, scalar=sb["D_t"][:, dc:dc + 1],
                in1=sb["y_s"][:, dc, :], op0=OP.mult, op1=OP.add), "s_dv")
            dvg = pr.inc("vector", lambda e, dc=dc, w=w: e.tensor_mul(
                out=sb["gated"][:, dc, :w], in0=t_b, in1=t_c), "s_dv")
            pr.wait("tensor", "s_dv", dvg)
            pr.wait("scalar", "s_dv", dvg)
        del dvS

    def emit_wout(p, c, ydst):
        w = TC
        ac_prev = {}
        for mb in range(4):
            buf = mb % 2
            if mb >= 2:
                pr.wait("gpsimd", "s_ac", ac_prev[mb * 2 - 3])
            gd = dma("gpsimd", "s_gd", sb["wout_t"][:, buf, :, :],
                     d[f"WoutT{p}"].ap().rearrange("(k p) m -> p k m", p=128)
                     [:, :, mb * 256:(mb + 1) * 256])
            pr.wait("tensor", "s_gd", gd)
            for m01 in (0, 1):
                m = mb * 2 + m01
                pp = ps["pA"] if m % 2 == 0 else ps["pB"]
                if m >= 2:
                    pr.wait("tensor", "s_ac", ac_prev[m - 2])
                def mk(e, buf=buf, m01=m01, pp=pp, w=w):
                    last = None
                    for k in range(DC):
                        last = e.matmul(
                            out=pp[:, :w],
                            lhsT=sb["wout_t"][:, buf, k, m01 * 128:(m01 + 1) * 128],
                            rhs=sb["gated"][:, k, :w],
                            start=(k == 0), stop=(k == DC - 1))
                    return last
                te = pr.inc("tensor", mk, "s_te")
                pr.wait("scalar", "s_te", te)
                ac_prev[m] = pr.inc("scalar", lambda e, m=m, pp=pp, w=w: e.copy(
                    out=sb["yout"][:, m, :w], in_=pp[:, :w]), "s_ac")
                pr.wait("sync", "s_ac", ac_prev[m])
                dma("sync", "s_st", fm(ydst)[:, m, c * TC:(c + 1) * TC],
                    sb["yout"][:, m, :w])
        pr.wait("sync", "s_st")

    def emit_mamba_phase(p):
        xs = xs_by_phase[p]
        ydst = d["y1d"] if p == 1 else d["y2d"]
        emit_ln(fm(xs)[:, :, 0:512], 512, dst_bf=sb["xn_bf"])
        pr.barrier()
        emit_win_xp(p, 512, None)
        pr.barrier()
        if DEV_DEBUG and p == 1:
            dma("sync", "s_st", d["dbg_pxp"].ap(), sb["xp_a"][:, :, 0:4])
            dma("sync", "s_st", d["dbg_pxb"].ap(), sb["xp_b"][:, :, 0:4])
            pr.wait("sync", "s_st"); pr.barrier()
        if p == 1:
            pr.inc("gpsimd", lambda e: e.memset(sb["carry"][:, :], 0.0), "s_gp")
            pr.barrier(("gpsimd", "vector"))
        for c in range(DEV_NCH):
            emit_mamba_chunk(p, c, xs, ydst)

    def emit_carry_exchange():
        st = dma("sync", "s_st", d["cc_ci"][:, :], sb["carry"][:, :])
        pr.wait("gpsimd", "s_st", st)
        cc = pr.inc("gpsimd", lambda e: e.collective_compute(
            "AllGather", OP.bypass,
            replica_groups=[[0, 1], [2, 3], [4, 5], [6, 7]],
            ins=[d["cc_ci"].ap()], outs=[d["cc_co"].ap()]), "s_cc", 1)
        pr.wait("gpsimd", "s_cc", cc)
        ld = dma("gpsimd", "s_gd",
                 sb["carrysel"][:, :].rearrange("p (r k) -> p r k", r=2),
                 d["cc_co"].ap().rearrange("(r p) k -> p r k", p=128))
        pr.wait("vector", "s_gd", ld)
        pr.inc("vector", lambda e: e.tensor_scalar_mul(
            out=sb["carry"][:, :], in0=sb["carrysel"][:, 0:DC * NST],
            scalar1=sb["mlo"][:, :]), "s_dv")
        pr.inc("vector", lambda e: e.scalar_tensor_tensor(
            out=sb["carry"][:, :], in0=sb["carrysel"][:, DC * NST:],
            scalar=sb["mhi"][:, :], in1=sb["carry"][:, :],
            op0=OP.mult, op1=OP.add), "s_dv")
        pr.barrier()

    def emit_combine():
        for c in range(NCH):
            w = TC
            emit_ln(fm(d["xnat"])[:, :, c * TC:(c + 1) * TC], w, dst_bf=None)
            for (ynm, mid, mrev) in (("y1d", "mid1", "mrev1"),
                                     ("y2d", "mid2", "mrev2")):
                rc = NCH - 1 - c
                pr.wait("sync", "s_dv")
                l1 = dma("sync", "s_ld", sb["yout"][:, :, :w],
                         fm(d[ynm])[:, :, c * TC:(c + 1) * TC])
                l2 = dma("sync", "s_ld", sb["yrev"][:, :, :w],
                         fm(d[ynm])[:, :, rc * TC:(rc + 1) * TC])
                pr.wait("vector", "s_ld", l2)
                for cc_ in range(CC):
                    pr.inc("vector", lambda e, cc_=cc_, mid=mid, w=w: e.tensor_scalar_mul(
                        out=sb["yout"][:, cc_, :w], in0=sb["yout"][:, cc_, :w],
                        scalar1=sb[mid][:, :]), "s_dv")
                    pr.inc("vector", lambda e, cc_=cc_, mrev=mrev, w=w: e.scalar_tensor_tensor(
                        out=sb["yout"][:, cc_, :w],
                        in0=sb["yrev"][:, cc_, :w][:, ::-1],
                        scalar=sb[mrev][:, :], in1=sb["yout"][:, cc_, :w],
                        op0=OP.mult, op1=OP.add), "s_dv")
                    pr.inc("vector", lambda e, cc_=cc_, w=w: e.scalar_tensor_tensor(
                        out=sb["xin"][:, cc_, :w], in0=sb["yout"][:, cc_, :w],
                        scalar=0.5, in1=sb["xin"][:, cc_, :w],
                        op0=OP.mult, op1=OP.add), "s_dv")
            for cc_ in range(CC):
                dv = pr.inc("vector", lambda e, cc_=cc_, w=w: e.tensor_add(
                    out=sb["xin"][:, cc_, :w], in0=sb["xin"][:, cc_, :w],
                    in1=sb["scr8"][:, cc_, :w]), "s_dv")
                pr.wait("sync", "s_dv", dv)
                dma("sync", "s_st", fm(d["x1d"])[:, cc_, c * TC:(c + 1) * TC],
                    sb["xin"][:, cc_, :w])
            pr.wait("sync", "s_st")
            pr.barrier()
        if STAGE_LAYERS >= 0:
            for c in range(NCH):
                w = TC
                emit_ln(fm(d["x1d"])[:, :, c * TC:(c + 1) * TC], w, dst_bf=None)
                for cc_ in range(CC):
                    pr.wait("sync", "s_dv")
                    dma("sync", "s_st", fm(d["za"])[:, cc_, c * TC:(c + 1) * TC],
                        sb["scr8"][:, cc_, :w])
                pr.wait("sync", "s_st")
                pr.barrier()


    TLORD = [2, 3, 4, 5, 6, 7, 0, 1]

    def emit_halo_exchange(zsrc, li):
        pr.wait("sync", "s_st")
        st = dma("sync", "s_st", d["cc_hi"].ap(), zsrc.ap()[:, T - HALO:T])
        pr.wait("gpsimd", "s_st", st)
        cc = pr.inc("gpsimd", lambda e: e.collective_compute(
            "AllGather", OP.bypass,
            replica_groups=[[0, 1], [2, 3], [4, 5], [6, 7]],
            ins=[d["cc_hi"].ap()], outs=[d["cc_ho"].ap()]), "s_cc", 1)
        pr.wait("gpsimd", "s_cc", cc)
        pr.inc("gpsimd", lambda e: e.memset(sb["zstage"][:, :, :], 0.0), "s_gp")
        pr.wait("gpsimd", "s_gp")
        ld = dma("gpsimd", "s_gd", sb["zstage"][:, :, 516 - HALO:516],
                 d["cc_ho"].ap()[0:D, :].rearrange("(c p) t -> p c t", p=128))
        pr.wait("vector", "s_gd", ld)
        for c in range(CC):
            pr.inc("vector", lambda e, c=c: e.tensor_scalar_mul(
                out=sb["zstage"][:, c, 516 - HALO:516],
                in0=sb["zstage"][:, c, 516 - HALO:516],
                scalar1=sb["mlo"][:, :]), "s_dv")
        pr.barrier()

    def emit_tok_exchange(zsrc):
        pr.wait("sync", "s_st")
        st = pr.inc("sync", lambda e: e.dma_start(
            out=d["cc_ti"].ap(), in_=zsrc.ap()[:, T - 1:T],
            allow_slow_non_contiguous=True), "s_st", 16)
        pr.wait("gpsimd", "s_st", st)
        cc = pr.inc("gpsimd", lambda e: e.collective_compute(
            "AllGather", OP.bypass,
            replica_groups=[[0, 1], [2, 3], [4, 5], [6, 7]],
            ins=[d["cc_ti"].ap()], outs=[d["cc_to"].ap()]), "s_cc", 1)
        pr.wait("gpsimd", "s_cc", cc)
        ld = pr.inc("gpsimd", lambda e: e.dma_start(
            out=sb["tok_sb"][:, :, 0:1],
            in_=d["cc_to"].ap()[0:D, :].rearrange("(c p) t -> p c t", p=128),
            allow_slow_non_contiguous=True), "s_gd", 16)
        pr.wait("vector", "s_gd", ld)
        for c in range(CC):
            pr.inc("vector", lambda e, c=c: e.tensor_scalar_mul(
                out=sb["tok_sb"][:, c, 0:1], in0=sb["tok_sb"][:, c, 0:1],
                scalar1=sb["mlo"][:, :]), "s_dv")
        pr.barrier()

    def shifted(src_tile, c, t0, w):
        if c < 4:
            return src_tile[:, c, t0:t0 + w]
        return src_tile[:, c, t0 - 1:t0 - 1 + w]

    def emit_qkv(li, src_tile, t0, goff, cur, want_q):
        s32 = sb["scratch32"]
        pr.wait("vector", "s_ac")
        pr.wait("vector", "s_te")
        for c in range(CC):
            pr.inc("vector", lambda e, c=c: e.tensor_copy(
                out=sb["zs_bf"][:, c, :], in_=shifted(src_tile, c, t0, FT)),
                "s_dv")
        pr.wait("sync", "s_dv")
        pr.wait("sync", "s_gp")
        for i, tab in enumerate(("rcq", "rsq", "rck", "rsk")):
            dma("sync", "s_ld", sb["scratch32"][:, 4 + i, :],
                d[tab].ap()[:, 512 + goff:512 + goff + FT])
        pr.wait("sync", "s_ld")
        pr.barrier(("vector", "tensor", "scalar", "gpsimd", "sync"))
        for qk in range(2):
            if qk == 0 and not want_q:
                continue
            for mb in range(4):
                buf = mb % 2
                pr.wait("gpsimd", "s_te")
                gd = dma("gpsimd", "s_gd", sb["wqk_t"][:, buf, :, :],
                         d[f"WqkT{li}"].ap().rearrange("(k p) m -> p k m", p=128)
                         [:, :, qk * D + mb * 256: qk * D + (mb + 1) * 256])
                pr.wait("tensor", "s_gd", gd)
                for m01 in (0, 1):
                    c = mb * 2 + m01
                    pp = ps["pA"] if c % 2 == 0 else ps["pB"]
                    pr.wait("tensor", "s_ac")
                    pr.wait("tensor", "s_dv")
                    pr.wait("scalar", "s_dv")
                    pr.wait("scalar", "s_gp")
                    def mk(e, buf=buf, m01=m01, pp=pp):
                        last = None
                        for k in range(CC):
                            last = e.matmul(
                                out=pp[:, :FT],
                                lhsT=sb["wqk_t"][:, buf, k,
                                                 m01 * 128:(m01 + 1) * 128],
                                rhs=sb["zs_bf"][:, k, :],
                                start=(k == 0), stop=(k == CC - 1))
                        return last
                    te = pr.inc("tensor", mk, "s_te")
                    pr.wait("scalar", "s_te", te)
                    a1 = pr.inc("scalar", lambda e, pp=pp: e.copy(
                        out=s32[:, 0, :], in_=pp[:, :FT]), "s_ac")
                    a2 = pr.inc("scalar", lambda e: e.activation(
                        out=s32[:, 1, :], in_=s32[:, 0, :], func=AF.Square),
                        "s_ac")
                    pr.wait("tensor", "s_ac", a2)
                    tn = pr.inc("tensor", lambda e: e.matmul(
                        out=ps["pR"][0:2, :FT], lhsT=sb["blk2_t"][:, :],
                        rhs=s32[:, 1, :], start=True, stop=True), "s_te")
                    pr.wait("scalar", "s_te", tn)
                    a3 = pr.inc("scalar", lambda e: e.activation(
                        out=sb["den2"][:, :], in_=ps["pR"][0:2, :FT],
                        func=AF.Sqrt), "s_ac")
                    pr.wait("vector", "s_ac", a3)
                    pr.inc("vector", lambda e: e.tensor_scalar_max(
                        out=sb["den2"][:, :], in0=sb["den2"][:, :],
                        scalar1=1e-12), "s_dv")
                    dvr = pr.inc("vector", lambda e: e.reciprocal(
                        out=sb["den2"][:, :], in_=sb["den2"][:, :]), "s_dv")
                    pr.wait("tensor", "s_dv", dvr)
                    tb2 = pr.inc("tensor", lambda e: e.matmul(
                        out=ps["pQ"][:, :FT], lhsT=sb["blk2b_t"][:, :],
                        rhs=sb["den2"][:, :], start=True, stop=True), "s_te")
                    sp = "qsp_t" if qk == 0 else "ksp_t"
                    ct = 0 if qk == 0 else 2
                    dv1 = pr.inc("vector", lambda e, sp=sp: e.tensor_scalar_mul(
                        out=s32[:, 2, :], in0=s32[:, 0, :],
                        scalar1=sb[sp][:, li:li + 1]), "s_dv")
                    pr.wait("gpsimd", "s_dv", dv1)
                    gp = pr.inc("gpsimd", lambda e, ct=ct: e.tensor_mul(
                        out=s32[:, 3, :], in0=s32[:, 2, :],
                        in1=sb["scratch32"][:, 4 + ct, :]), "s_gp")
                    pr.wait("tensor", "s_dv", dv1)
                    tr = pr.inc("tensor", lambda e: e.matmul(
                        out=ps["pR"][:, :FT], lhsT=sb["rotm_t"][:, :],
                        rhs=s32[:, 2, :], start=True, stop=True), "s_te")
                    pr.wait("vector", "s_te", tr)
                    pr.inc("vector", lambda e, ct=ct: e.tensor_mul(
                        out=s32[:, 0, :], in0=ps["pR"][:, :FT],
                        in1=sb["scratch32"][:, 4 + ct + 1, :]), "s_dv")
                    pr.wait("vector", "s_gp", gp)
                    dvu = pr.inc("vector", lambda e: e.tensor_add(
                        out=s32[:, 0, :], in0=s32[:, 0, :], in1=s32[:, 3, :]),
                        "s_dv")
                    pr.wait("tensor", "s_dv", dvu)
                    pr.wait("vector", "s_te", tb2)
                    dst = sb["q_bf"] if qk == 0 else None
                    def mkst(e, c=c, qk=qk, cur=cur):
                        out = (sb["q_bf"][:, c, :] if qk == 0
                               else sb["kbuf"][:, cur, c, :])
                        return e.scalar_tensor_tensor(
                            out=out, in0=s32[:, 0, :], scalar=1.0,
                            in1=ps["pQ"][:, :FT], op0=OP.mult, op1=OP.mult)
                    dvf = pr.inc("vector", mkst, "s_dv")
                    pr.wait("tensor", "s_dv", dvf)
        pr.wait("gpsimd", "s_te")
        gd = dma("gpsimd", "s_gd", sb["wv_sb"][:, :, :],
                 d[f"Wv{li}"].ap().rearrange("(k p) m -> p k m", p=128))
        pr.wait("tensor", "s_gd", gd)
        for j in range(4):
            for nh in (0, 1):
                pp = ps["pA"] if nh == 0 else ps["pB"]
                pr.wait("tensor", "s_ac")
                def mkv(e, j=j, nh=nh, pp=pp):
                    last = None
                    for k in range(CC):
                        last = e.matmul(
                            out=pp[:, :FT],
                            lhsT=sb["zs_bf"][:, k, j * 128:(j + 1) * 128],
                            rhs=sb["wv_sb"][:, k, nh * 512:(nh + 1) * 512],
                            start=(k == 0), stop=(k == CC - 1))
                    return last
                te = pr.inc("tensor", mkv, "s_te")
                pr.wait("scalar", "s_te", te)
                pr.inc("scalar", lambda e, j=j, nh=nh, pp=pp, cur=cur: e.copy(
                    out=sb["vbuf"][:, cur, j, nh * 512:(nh + 1) * 512],
                    in_=pp[:, :FT]), "s_ac")
        pr.barrier(("tensor", "scalar", "vector", "gpsimd", "sync"))

    def emit_attn_tile(li, ti, cur, zsrc, zdst):
        T0 = ti * FT
        s32 = sb["scratch32"]
        pr.wait("sync", "s_dv")
        pr.wait("sync", "s_ac")
        if ti == 0:
            l = dma("sync", "s_ld", sb["zt"][:, :, 1:513],
                    fm(zsrc)[:, :, 0:FT])
            pr.wait("vector", "s_ld", l)
            dv = pr.inc("vector", lambda e: e.tensor_copy(
                out=sb["zt"][:, :, 0:1], in_=sb["zstage"][:, :, 515:516]),
                "s_dv")
        else:
            l = dma("sync", "s_ld", sb["zt"][:, :, 0:513],
                    fm(zsrc)[:, :, T0 - 1:T0 + FT])
        pr.wait("sync", "s_ld")
        pr.barrier(("sync", "vector", "tensor", "scalar", "gpsimd"))
        pr.wait("sync", "s_dv")
        dma("sync", "s_ld", sb["mskS_sb"][:, :, :],
            d["maskS"].ap().rearrange("(t j p) k -> p (t j) k", p=128, j=4)
            [:, ti * 4:(ti + 1) * 4, :])
        dma("sync", "s_ld", sb["mskP_sb"][:, :, :],
            d["maskP"].ap().rearrange("(t x p) k -> p (t x) k", p=128, x=12)
            [:, ti * 12:(ti + 1) * 12, :])
        pr.wait("sync", "s_ld")
        emit_qkv(li, sb["zt"], 1, T0, cur, want_q=True)
        def ktslice(kt_abs, arr, cur):
            if kt_abs < 0:
                return arr[:, 1 - cur], kt_abs + 4
            return arr[:, cur], kt_abs
        for h in range(H):
            ch, hh = h // 2, h % 2
            for j in range(4):
                def mks(e, ch=ch, hh=hh, j=j, cur=cur):
                    last = None
                    for kt in range(3):
                        buf, jj = (1 - cur, j - 2 + kt + 4) if j - 2 + kt < 0 \
                            else (cur, j - 2 + kt)
                        last = e.matmul(
                            out=ps["pS"][:, kt * 128:(kt + 1) * 128],
                            lhsT=sb["q_bf"][hh * 64:(hh + 1) * 64, ch,
                                            j * 128:(j + 1) * 128],
                            rhs=sb["kbuf"][hh * 64:(hh + 1) * 64, buf, ch,
                                           jj * 128:(jj + 1) * 128],
                            start=True, stop=True)
                    return last
                te = pr.inc("tensor", mks, "s_te")
                pr.wait("vector", "s_te", te)
                pr.inc("vector", lambda e, ti=ti, j=j: e.tensor_add(
                    out=s32[:, 0, 0:384], in0=ps["pS"][:, 0:384],
                    in1=sb["mskS_sb"][:, j, :]), "s_dv")
                dv = pr.inc("vector", lambda e, h=h, j=j: e.reduce_max(
                    out=sb["m_cols"][:, h * 4 + j:h * 4 + j + 1],
                    in_=s32[:, 0, 0:384], axis=AX.X), "s_dv")
                pr.wait("tensor", "s_dv", dv)
        dvt = pr.cnt["s_dv"]
        pr.wait("tensor", "s_dv", dvt)
        te = pr.inc("tensor", lambda e: e.transpose(
            out=ps["pR"][0:64, 0:128], in_=sb["m_cols"][:, :],
            identity=sb["idf32_t"][:, :]), "s_te")
        pr.wait("scalar", "s_te", te)
        ac = pr.inc("scalar", lambda e: e.copy(
            out=sb["mrow_sb"][:, :], in_=ps["pR"][0:64, 0:128]), "s_ac")
        pr.wait("sync", "s_ac", ac)
        st = dma("sync", "s_st", d["mrow"].ap(), sb["mrow_sb"].ap())
        pr.wait("sync", "s_st", st)
        lb = dma("sync", "s_ld", sb["mrowr"][:, :], d["mrow"].ap())
        pr.wait("tensor", "s_ld", lb)
        pr.barrier(("tensor", "scalar", "vector", "sync"))
        for h in range(H):
            ch, hh = h // 2, h % 2
            pp_o = ps["pV"] if h % 2 == 0 else ps["pW"]
            for j in range(4):
                def mkst(e, ch=ch, hh=hh, j=j, h=h, cur=cur):
                    for kt in range(3):
                        buf, jj = (1 - cur, j - 2 + kt + 4) if j - 2 + kt < 0 \
                            else (cur, j - 2 + kt)
                        e.matmul(
                            out=ps["pS2"][:, kt * 128:(kt + 1) * 128],
                            lhsT=sb["kbuf"][hh * 64:(hh + 1) * 64, buf, ch,
                                            jj * 128:(jj + 1) * 128],
                            rhs=sb["q_bf"][hh * 64:(hh + 1) * 64, ch,
                                           j * 128:(j + 1) * 128],
                            start=True, stop=False)
                        last = e.matmul(
                            out=ps["pS2"][:, kt * 128:(kt + 1) * 128],
                            lhsT=sb["negb_t"][:, :],
                            rhs=sb["mrowr"][:, (h * 4 + j) * 128:
                                            (h * 4 + j + 1) * 128],
                            start=False, stop=True)
                    return last
                te = pr.inc("tensor", mkst, "s_te")
                pr.wait("scalar", "s_te", te)
                ac = pr.inc("scalar", lambda e: e.activation(
                    out=sb["pt_bf"][:, :], in_=ps["pS2"][:, 0:384],
                    func=AF.Exp, scale=QK_SCALE), "s_ac")
                pr.wait("vector", "s_ac", ac)
                dv = pr.inc("vector", lambda e, j=j: e.tensor_mul(
                    out=sb["pt_bf"][:, :], in0=sb["pt_bf"][:, :],
                    in1=sb["mskP_sb"][:, j * 3:(j + 1) * 3, :].rearrange(
                        "p a b -> p (a b)")), "s_dv")
                pr.wait("tensor", "s_dv", dv)
                def mkpv(e, ch=ch, hh=hh, j=j, h=h, cur=cur, pp_o=pp_o):
                    for kt in range(3):
                        buf, jj = (1 - cur, j - 2 + kt + 4) if j - 2 + kt < 0 \
                            else (cur, j - 2 + kt)
                        e.matmul(
                            out=pp_o[0:64, j * 128:(j + 1) * 128],
                            lhsT=sb["vbuf"][:, buf, jj, h * 64:(h + 1) * 64],
                            rhs=sb["pt_bf"][:, kt * 128:(kt + 1) * 128],
                            start=(kt == 0), stop=(kt == 2))
                        last = e.matmul(
                            out=pp_o[64:65, j * 128:(j + 1) * 128],
                            lhsT=sb["onescb_t"][:, :],
                            rhs=sb["pt_bf"][:, kt * 128:(kt + 1) * 128],
                            start=(kt == 0), stop=(kt == 2))
                    return last
                te = pr.inc("tensor", mkpv, "s_te")
                pr.wait("scalar", "s_te", te)
            pr.inc("scalar", lambda e, ch=ch, hh=hh, pp_o=pp_o: e.copy(
                out=sb["o32"][hh * 64:(hh + 1) * 64, ch, :],
                in_=pp_o[0:64, :FT]), "s_ac")
            ac = pr.inc("scalar", lambda e, hh=hh, pp_o=pp_o: e.copy(
                out=sb["den"][0:1, hh * FT:(hh + 1) * FT],
                in_=pp_o[64:65, :FT]), "s_ac")
            pr.wait("tensor", "s_ac", ac)
            if hh == 1:
                pr.wait("vector", "s_ac", ac)
                dv = pr.inc("vector", lambda e: e.reciprocal(
                    out=sb["den"][0:1, :], in_=sb["den"][0:1, :]), "s_dv")
                pr.wait("tensor", "s_dv", dv)
                def mkbc(e):
                    e.matmul(out=ps["pQ"][:, :FT], lhsT=sb["blkA_t"][:, :],
                             rhs=sb["den"][0:1, 0:FT], start=True, stop=False)
                    return e.matmul(out=ps["pQ"][:, :FT],
                                    lhsT=sb["blkB_t"][:, :],
                                    rhs=sb["den"][0:1, FT:2 * FT],
                                    start=False, stop=True)
                te = pr.inc("tensor", mkbc, "s_te")
                pr.wait("vector", "s_te", te)
                dv = pr.inc("vector", lambda e, ch=ch: e.scalar_tensor_tensor(
                    out=sb["zs_bf"][:, ch, :], in0=sb["o32"][:, ch, :],
                    scalar=1.0, in1=ps["pQ"][:, :FT],
                    op0=OP.mult, op1=OP.mult), "s_dv")
                pr.wait("tensor", "s_dv", dv)
        if DEV_DEBUG and li == 0 and ti == 3:
            for nm, t in (("dbg_q", "q_bf"), ("dbg_k", "kbuf"),
                          ("dbg_v", "vbuf"), ("dbg_o", "o32"),
                          ("dbg_zs", "zs_bf"), ("dbg_pt", "pt_bf"),
                          ("dbg_den", "den")):
                dma("sync", "s_st", d[nm].ap(), sb[t].ap())
            pr.wait("sync", "s_st")
            pr.barrier()
        pr.barrier(("tensor", "scalar", "vector", "gpsimd", "sync"))
        for m in range(CC):
            buf = m % 2
            pr.wait("gpsimd", "s_te")
            gd = dma("gpsimd", "s_gd", sb["wo_t"][:, buf, :, :],
                     d[f"Wo{li}"].ap().rearrange("(k p) m -> p k m", p=128)
                     [:, :, m * 128:(m + 1) * 128])
            pr.wait("tensor", "s_gd", gd)
            pr.wait("tensor", "s_dv")
            def mko(e, buf=buf):
                last = None
                for k in range(CC):
                    last = e.matmul(out=ps["pA"][:, :FT],
                                    lhsT=sb["wo_t"][:, buf, k, :],
                                    rhs=sb["zs_bf"][:, k, :],
                                    start=(k == 0), stop=(k == CC - 1))
                return last
            te = pr.inc("tensor", mko, "s_te")
            pr.wait("vector", "s_te", te)
            dv = pr.inc("vector", lambda e, m=m: e.tensor_add(
                out=sb["o32"][:, m, :], in0=ps["pA"][:, :FT],
                in1=sb["zt"][:, m, 1:513]), "s_dv")
            pr.wait("tensor", "s_dv", dv)
            pr.wait("sync", "s_dv", dv)
            dma("sync", "s_st", fm(zdst)[:, m, T0:T0 + FT], sb["o32"][:, m, :])
        pr.wait("sync", "s_st")
        pr.barrier()

    def emit_ffn_tile(li, ti, zsrc, zdst, last):
        T0 = ti * FT
        s32 = sb["scratch32"]
        rows = sb["rows"]
        Rr = lambda i: rows[0:1, i * TC: i * TC + FT]
        pr.wait("sync", "s_dv")
        pr.wait("sync", "s_ac")
        if ti == 0:
            l = dma("sync", "s_ld", sb["zt"][:, :, 1:513], fm(zsrc)[:, :, 0:FT])
            pr.wait("vector", "s_ld", l)
            pr.inc("vector", lambda e: e.tensor_copy(
                out=sb["zt"][:, :, 0:1], in_=sb["tok_sb"][:, :, 0:1]), "s_dv")
        else:
            l = dma("sync", "s_ld", sb["zt"][:, :, 0:513],
                    fm(zsrc)[:, :, T0 - 1:T0 + FT])
        if last:
            dma("sync", "s_ld", sb["o32"][:, :, :], fm(d["x1d"])[:, :, T0:T0 + FT])
        pr.wait("sync", "s_ld")
        pr.barrier(("sync", "vector", "tensor", "scalar", "gpsimd"))
        sq = None
        for c in range(CC):
            sq = pr.inc("scalar", lambda e, c=c: e.activation(
                out=s32[:, c, :], in_=shifted(sb["zt"], c, 1, FT),
                func=AF.Square), "s_ac")
        pr.wait("tensor", "s_ac", sq)
        def mk_stats(e):
            last = None
            for c in range(CC):
                last = e.matmul(out=ps["pS"][0:1, :FT], lhsT=sb["onesc_t"][:, :],
                                rhs=shifted(sb["zt"], c, 1, FT),
                                start=(c == 0), stop=(c == CC - 1))
            for c in range(CC):
                last = e.matmul(out=ps["pS2"][0:1, :FT], lhsT=sb["onesc_t"][:, :],
                                rhs=s32[:, c, :],
                                start=(c == 0), stop=(c == CC - 1))
            return last
        st = pr.inc("tensor", mk_stats, "s_te")
        pr.wait("vector", "s_te", st)
        pr.inc("vector", lambda e: e.tensor_scalar_mul(
            out=Rr(0), in0=ps["pS"][0:1, :FT], scalar1=1.0 / D), "s_dv")
        pr.inc("vector", lambda e: e.tensor_scalar_mul(
            out=Rr(1), in0=ps["pS2"][0:1, :FT], scalar1=1.0 / D), "s_dv")
        pr.inc("vector", lambda e: e.tensor_mul(out=Rr(2), in0=Rr(0), in1=Rr(0)), "s_dv")
        dv = pr.inc("vector", lambda e: e.tensor_sub(out=Rr(1), in0=Rr(1), in1=Rr(2)), "s_dv")
        pr.wait("scalar", "s_dv", dv)
        ac = pr.inc("scalar", lambda e: e.activation(
            out=Rr(2), in_=Rr(1), func=AF.Sqrt, bias=EPS), "s_ac")
        pr.wait("vector", "s_ac", ac)
        dv = pr.inc("vector", lambda e: e.reciprocal(out=Rr(3), in_=Rr(2)), "s_dv")
        pr.wait("tensor", "s_dv", dv)
        def mk_bc(e):
            e.matmul(out=ps["pS"][:, :FT], lhsT=sb["ones1_t"][:, :], rhs=Rr(0),
                     start=True, stop=True)
            return e.matmul(out=ps["pS2"][:, :FT], lhsT=sb["ones1_t"][:, :],
                            rhs=Rr(3), start=True, stop=True)
        bc = pr.inc("tensor", mk_bc, "s_te")
        pr.wait("vector", "s_te", bc)
        for c in range(CC):
            pr.inc("vector", lambda e, c=c: e.tensor_sub(
                out=s32[:, c, :], in0=shifted(sb["zt"], c, 1, FT),
                in1=ps["pS"][:, :FT]), "s_dv")
            pr.inc("vector", lambda e, c=c: e.tensor_mul(
                out=s32[:, c, :], in0=s32[:, c, :], in1=ps["pS2"][:, :FT]),
                "s_dv")
            dv = pr.inc("vector", lambda e, c=c: e.tensor_scalar(
                out=sb["znf_bf"][:, c, :], in0=s32[:, c, :],
                scalar1=sb["lng_t"][:, c:c + 1], scalar2=sb["lnbf_t"][:, c:c + 1],
                op0=OP.mult, op1=OP.add), "s_dv")
        pr.barrier(("tensor", "scalar", "vector", "gpsimd", "sync"))
        NPR = DFP // 128
        for prr in range(NPR):
            buf = prr % 2
            pr.wait("gpsimd", "s_te")
            gd = dma("gpsimd", "s_gd", sb["w1_t"][:, buf, :, :],
                     d[f"W1T{li}"].ap().rearrange("(k p) m -> p k m", p=128)
                     [:, :, prr * 256:(prr + 1) * 256])
            pr.wait("tensor", "s_gd", gd)
            pr.wait("tensor", "s_ac")
            pr.wait("tensor", "s_dv")
            def mk1(e, buf=buf):
                for k in range(CC):
                    e.matmul(out=ps["pA"][:, :FT],
                             lhsT=sb["w1_t"][:, buf, k, 0:128],
                             rhs=sb["znf_bf"][:, k, :],
                             start=(k == 0), stop=(k == CC - 1))
                last = None
                for k in range(CC):
                    last = e.matmul(out=ps["pB"][:, :FT],
                                    lhsT=sb["w1_t"][:, buf, k, 128:256],
                                    rhs=sb["znf_bf"][:, k, :],
                                    start=(k == 0), stop=(k == CC - 1))
                return last
            te = pr.inc("tensor", mk1, "s_te")
            pr.wait("scalar", "s_te", te)
            ac = pr.inc("scalar", lambda e, prr=prr: e.activation(
                out=s32[:, 0, :], in_=ps["pB"][:, :FT], func=AF.Gelu,
                bias=sb["b1p_t"][:, 2 * prr + 1:2 * prr + 2]), "s_ac")
            pr.wait("vector", "s_ac", ac)
            dv = pr.inc("vector", lambda e, prr=prr: e.scalar_tensor_tensor(
                out=sb["ag_bf"][:, prr, :], in0=ps["pA"][:, :FT],
                scalar=sb["b1p_t"][:, 2 * prr:2 * prr + 1], in1=s32[:, 0, :],
                op0=OP.add, op1=OP.mult), "s_dv")
        pr.barrier(("tensor", "scalar", "vector", "gpsimd"))
        for m in range(CC):
            pr.wait("gpsimd", "s_te")
            gd = dma("gpsimd", "s_gd", sb["w2_t"][:, :, :],
                     d[f"W2T{li}"].ap().rearrange("(k p) m -> p k m", p=128)
                     [:, :, m * 128:(m + 1) * 128])
            pr.wait("tensor", "s_gd", gd)
            pr.wait("tensor", "s_dv")
            def mk2(e):
                last = None
                for k in range(NPR):
                    last = e.matmul(
                        out=ps["pA"][:, :FT],
                        lhsT=sb["w2_t"][:, k, :],
                        rhs=sb["ag_bf"][:, k, :],
                        start=(k == 0), stop=(k == NPR - 1))
                return last
            te = pr.inc("tensor", mk2, "s_te")
            pr.wait("vector", "s_te", te)
            dv = pr.inc("vector", lambda e, m=m: e.scalar_tensor_tensor(
                out=s32[:, m, :], in0=ps["pA"][:, :FT],
                scalar=sb["b2_t"][:, m:m + 1], in1=sb["zt"][:, m, 1:513],
                op0=OP.add, op1=OP.add), "s_dv")
            if last:
                dv = pr.inc("vector", lambda e, m=m: e.tensor_add(
                    out=s32[:, m, :], in0=s32[:, m, :], in1=sb["o32"][:, m, :]),
                    "s_dv")
            pr.wait("tensor", "s_dv", dv)
            pr.wait("sync", "s_dv", dv)
            dst = d["out"] if last else zdst
            dma("sync", "s_st", fm(dst)[:, m, T0:T0 + FT], s32[:, m, :])
        pr.wait("sync", "s_st")
        pr.barrier()

    def emit_ffn_all():
        for li in range(STAGE_LAYERS):
            zsrc, zdst = d["za"], d["zb"]
            dma("sync", "s_ld", sb["lng_t"][:, :], d[f"lng{li}"][:, :])
            dma("sync", "s_ld", sb["lnbf_t"][:, :], d[f"lnbf{li}"][:, :])
            dma("sync", "s_ld", sb["b1p_t"][:, :], d[f"b1p{li}"][:, :])
            dma("sync", "s_ld", sb["b2_t"][:, :], d[f"b2{li}"][:, :])
            pr.wait("sync", "s_ld")
            pr.barrier()
            emit_halo_exchange(zsrc, li)
            for pos, ti in enumerate(TLORD):
                cur = pos % 2
                if ti == 2:
                    pr.wait("sync", "s_dv")
                    pr.wait("sync", "s_ac")
                    l = dma("sync", "s_ld", sb["zt"][:, :, 0:513],
                            fm(zsrc)[:, :, FT - 1:2 * FT])
                    pr.wait("sync", "s_ld", l)
                    pr.barrier(("sync", "vector", "tensor", "scalar", "gpsimd"))
                    emit_qkv(li, sb["zt"], 1, FT, 1 - cur, want_q=False)
                elif ti == 0:
                    emit_qkv(li, sb["zstage"], 4, -FT, 1 - cur, want_q=False)
                emit_attn_tile(li, ti, cur, zsrc, zdst)
            emit_tok_exchange(zdst)
            last = (li == DEPTH - 1) and (STAGE_LAYERS == DEPTH)
            for ti in [1, 2, 3, 4, 5, 6, 7, 0]:
                emit_ffn_tile(li, ti, zdst, zsrc, last)

    emit_load_consts()
    emit_load_phase_weights(1)
    emit_mamba_phase(1)
    if 2 in DEV_PHASES:
        emit_carry_exchange()
        emit_load_phase_weights(2)
        emit_mamba_phase(2)
        pr.barrier()
        emit_combine()

    if STAGE_LAYERS < 0:
        for c in range(NCH):
            pr.wait("sync", "s_st")
            l = dma("sync", "s_ld", sb["xin"][:, :, :],
                    fm(d[DEV_OUT])[:, :, c * TC:(c + 1) * TC])
            pr.wait("sync", "s_ld", l)
            dma("sync", "s_st", fm(d["out"])[:, :, c * TC:(c + 1) * TC],
                sb["xin"][:, :, :])
        pr.wait("sync", "s_st")
    else:
        emit_ffn_all()
        if STAGE_LAYERS < DEPTH:
            for c in range(NCH):
                pr.wait("sync", "s_st")
                l = dma("sync", "s_ld", sb["xin"][:, :, :],
                        fm(d[DEV_OUT if DEV_OUT != "x1d" else "za"])[:, :, c * TC:(c + 1) * TC])
                pr.wait("sync", "s_ld", l)
                dma("sync", "s_st", fm(d["out"])[:, :, c * TC:(c + 1) * TC],
                    sb["xin"][:, :, :])
        pr.wait("sync", "s_st")
    pr.barrier()

    with nc.Block() as block:
        @block.sync
        def _(e):
            for fn in pr.q["sync"]:
                fn(e)

        @block.tensor
        def _(e):
            for fn in pr.q["tensor"]:
                fn(e)

        @block.vector
        def _(e):
            for fn in pr.q["vector"]:
                fn(e)

        @block.scalar
        def _(e):
            for fn in pr.q["scalar"]:
                fn(e)

        @block.gpsimd
        def _(e):
            for fn in pr.q["gpsimd"]:
                fn(e)

    build_program._sb = sb
    build_program._nc = nc
    ctx.close()
    return nc


def _conv_diag(Wc):
    out = np.zeros((128, DC * 4 * 128), np.float32)
    for dc in range(DC):
        for k in range(4):
            blk = np.diag(Wc[dc * 128:(dc + 1) * 128, k])
            out[:, (dc * 4 + k) * 128:(dc * 4 + k + 1) * 128] = blk
    return out


def _p128(v):
    return np.ascontiguousarray(v.reshape(DC, 128).T.astype(np.float32))


def _host_core_inputs(inputs, b, h):
    x = np.asarray(inputs["L0_attn_x"][b], np.float32)
    xa = x[0:T]
    xb = x[T:2 * T]
    im = {}
    z508 = np.zeros((508, D), np.float32)
    if h == 0:
        own = xa
        xs1 = np.concatenate([z508, np.zeros((4, D), np.float32), own], 0)
        ext = np.concatenate([own, xb[0:4]], 0)
        xs2 = np.concatenate([z508, ext[::-1]], 0)
        w1, w2 = "f_", "b_"
        m_id1, m_id2 = 1.0, 0.0
        m_lo, m_hi = 0.0, 1.0
    else:
        own = xb
        xs1 = np.concatenate([z508, np.zeros((4, D), np.float32), own[::-1]], 0)
        ext = np.concatenate([xa[T - 4:T], own], 0)
        xs2 = np.concatenate([z508, ext], 0)
        w1, w2 = "b_", "f_"
        m_id1, m_id2 = 0.0, 1.0
        m_lo, m_hi = 1.0, 0.0
    im["xs1"] = np.ascontiguousarray(xs1.T)
    im["xs2"] = np.ascontiguousarray(xs2.T)
    im["xnat"] = np.ascontiguousarray(own.T)
    for p, pref in ((1, w1), (2, w2)):
        g = lambda nm: np.asarray(inputs[f"L0_attn_{pref}{nm}"], np.float32)
        im[f"WinT{p}"] = np.ascontiguousarray(g("W_in").T)
        im[f"Wconvd{p}"] = _conv_diag(g("W_conv"))
        im[f"WxT{p}"] = np.ascontiguousarray(g("W_x").T)
        im[f"WdtT{p}"] = np.ascontiguousarray(g("W_dt").T)
        im[f"WoutT{p}"] = np.ascontiguousarray(g("W_out").T)
        im[f"Alog{p}"] = np.ascontiguousarray(
            g("A_log").reshape(DC, 128, NST).transpose(1, 0, 2).reshape(128, DC * NST))
        im[f"bconv{p}"] = _p128(g("b_conv"))
        im[f"bdt{p}"] = _p128(g("b_dt"))
        im[f"Dp{p}"] = _p128(g("D"))
    im["ln_w"] = np.ascontiguousarray(
        np.asarray(inputs["L0_attn_ln_w"], np.float32).reshape(CC, 128).T)
    im["ln_b"] = np.ascontiguousarray(
        np.asarray(inputs["L0_attn_ln_b"], np.float32).reshape(CC, 128).T)
    im["ones1"] = np.ones((1, 128), np.float32)
    im["ones_c"] = np.ones((128, 1), np.float32)
    cv = np.zeros((128, 16), np.float32)
    cv[:, :8] = np.asarray([EPS, 1.0, -1.0, QK_SCALE, 0.5, 1.0 / D, 0.0, 2.0])
    im["cvals"] = cv
    im["onesb"] = np.ones((1, 128), ml_dtypes.bfloat16)
    for nm, v in (("m_id1", m_id1), ("m_rev1", 1.0 - m_id1),
                  ("m_id2", m_id2), ("m_rev2", 1.0 - m_id2),
                  ("m_lo", m_lo), ("m_hi", m_hi)):
        im[nm] = np.full((128, 1), v, np.float32)
    if STAGE_LAYERS >= 0:
        im.update(_ffn_host_inputs(inputs, h))
    return im




def _ffn_host_inputs(inputs, h):
    im = {}
    asc = np.ascontiguousarray
    for li in range(DEPTH):
        Wqkv = np.asarray(inputs["L0_ffn_Wqkv"][li], np.float32)
        im[f"WqkT{li}"] = asc(Wqkv[:, 0:2 * D])
        im[f"Wv{li}"] = asc(Wqkv[:, 2 * D:3 * D])
        im[f"Wo{li}"] = asc(np.asarray(inputs["L0_ffn_Wo"][li], np.float32))
        im[f"lng{li}"] = asc(np.asarray(inputs["L0_ffn_ln_g"][li],
                                        np.float32).reshape(CC, 128).T)
        im[f"lnbf{li}"] = asc(np.asarray(inputs["L0_ffn_ln_b"][li],
                                         np.float32).reshape(CC, 128).T)
        W1 = np.asarray(inputs["L0_ffn_W1"][li], np.float32)
        a = np.zeros((D, DFP), np.float32); a[:, :DFI] = W1[:, :DFI]
        g = np.zeros((D, DFP), np.float32); g[:, :DFI] = W1[:, DFI:2 * DFI]
        NPR = DFP // 128
        w1p = np.zeros((D, 2 * DFP), np.float32)
        for i in range(NPR):
            w1p[:, (2 * i) * 128:(2 * i + 1) * 128] = a[:, i * 128:(i + 1) * 128]
            w1p[:, (2 * i + 1) * 128:(2 * i + 2) * 128] = g[:, i * 128:(i + 1) * 128]
        im[f"W1T{li}"] = asc(w1p)
        b1 = np.asarray(inputs["L0_ffn_b1"][li], np.float32)
        b1a = np.zeros(DFP, np.float32); b1a[:DFI] = b1[:DFI]
        b1g = np.zeros(DFP, np.float32); b1g[:DFI] = b1[DFI:]
        b1p = np.zeros((128, 2 * NPR), np.float32)
        for i in range(NPR):
            b1p[:, 2 * i] = b1a[i * 128:(i + 1) * 128]
            b1p[:, 2 * i + 1] = b1g[i * 128:(i + 1) * 128]
        im[f"b1p{li}"] = b1p
        W2 = np.asarray(inputs["L0_ffn_W2"][li], np.float32)
        w2p = np.zeros((DFP, D), np.float32); w2p[:DFI] = W2
        im[f"W2T{li}"] = asc(w2p)
        im[f"b2{li}"] = asc(np.asarray(inputs["L0_ffn_b2"][li],
                                       np.float32).reshape(CC, 128).T)
    qs = np.asarray(inputs["L0_ffn_q_scale"], np.float32)
    ks = np.asarray(inputs["L0_ffn_k_scale"], np.float32)
    dd = np.arange(128) % 64
    im["qsp"] = asc(qs[:, dd].T)
    im["ksp"] = asc(ks[:, dd].T)
    G0 = h * T
    t = np.arange(G0 - 512, G0 + T, dtype=np.float64)
    half = np.arange(0, DH, 2, dtype=np.float64)
    inv_freq = 1.0 / 10000.0 ** (half / DH)
    fr = t[:, None] * inv_freq[None, :]
    freqs = np.concatenate([fr, fr], -1)
    base = (half + 0.4 * DH) / (1.4 * DH)
    power = (t - TG // 2) / (WIN // 2)
    scale = base[None, :] ** power[:, None]
    scale = np.concatenate([scale, scale], -1)
    cosv = np.cos(freqs); sinv = np.sin(freqs)
    sgn = np.where(dd < 32, -1.0, 1.0)
    im["rcq"] = asc((cosv[:, dd] * scale[:, dd]).T.astype(np.float32))
    im["rsq"] = asc((sinv[:, dd] * scale[:, dd] * sgn[None, :]).T.astype(np.float32))
    im["rck"] = asc((cosv[:, dd] / scale[:, dd]).T.astype(np.float32))
    im["rsk"] = asc((sinv[:, dd] / scale[:, dd] * sgn[None, :]).T.astype(np.float32))
    NEGF = np.float32(NEG)
    mS = np.zeros((NFT, 4, 128, 384), np.float32)
    mP = np.zeros((NFT, 4, 3, 128, 128), np.float32)
    for ti in range(NFT):
        for j in range(4):
            itl = G0 + ti * FT + j * 128 + np.arange(128)
            jbl = G0 + ti * FT + j * 128 - 256 + np.arange(384)
            keep = ((jbl[None, :] <= itl[:, None])
                    & (itl[:, None] <= jbl[None, :] + WIN)
                    & (jbl[None, :] >= 0))
            mS[ti, j] = np.where(keep, 0.0, NEGF)
            for kt in range(3):
                mP[ti, j, kt] = keep[:, kt * 128:(kt + 1) * 128].T
    im["maskS"] = asc(mS.reshape(NFT * 4 * 128, 384))
    im["maskP"] = asc(mP.reshape(NFT * 4 * 3 * 128, 128).astype(ml_dtypes.bfloat16))
    blk2 = np.zeros((128, 2), np.float32)
    blk2[0:64, 0] = 1.0; blk2[64:128, 1] = 1.0
    im["blk2"] = blk2
    im["blk2b"] = asc(blk2.T)
    blkA = np.zeros((1, 128), np.float32); blkA[0, 0:64] = 1.0
    blkB = np.zeros((1, 128), np.float32); blkB[0, 64:128] = 1.0
    im["blkA"] = blkA; im["blkB"] = blkB
    im["negb"] = np.full((1, 128), -1.0, ml_dtypes.bfloat16)
    im["onescb"] = np.ones((128, 1), ml_dtypes.bfloat16)
    im["idf32"] = np.eye(128, dtype=np.float32)
    rotm = np.zeros((128, 128), np.float32)
    for m in range(128):
        srcp = m + 32 if (m % 64) < 32 else m - 32
        rotm[srcp, m] = 1.0
    im["rotm"] = rotm
    return im

_NC_CACHE = {}


def kernel(**inputs):
    key = (STAGE_LAYERS, DEV_OUT, DEV_DEBUG, DEV_NCH, tuple(DEV_PHASES))
    if key not in _NC_CACHE:
        _NC_CACHE[key] = build_program()
    nc = _NC_CACHE[key]
    in_maps = []
    for b in range(B):
        for h in range(2):
            in_maps.append(_host_core_inputs(inputs, b, h))
    res = run_bass_kernel_spmd(nc, in_maps, core_ids=list(range(8)))
    out = np.zeros((B, TG, D), np.float32)
    for ci, (b, h) in enumerate((b, h) for b in range(B) for h in range(2)):
        o = np.asarray(res.results[ci]["out"])
        out[b, h * T:(h + 1) * T] = o.T
    if DEV_DEBUG:
        kernel._dbg = res.results
    return out


# version 02
# speedup vs baseline: 1.3065x; 1.3065x over previous
from contextlib import ExitStack
from functools import reduce as _reduce

import numpy as np
import ml_dtypes
import concourse.bass as bass
import concourse.mybir as mybir
from concourse.bass_utils import run_bass_kernel_spmd

F32 = mybir.dt.float32
BF16 = mybir.dt.bfloat16
AF = mybir.ActivationFunctionType
OP = mybir.AluOpType
AX = mybir.AxisListType

B = 4
TG = 8192
T = 4096
D = 1024
DI = 2048
NST = 16
RNK = 64
H = 16
DH = 64
WIN = 256
DEPTH = 6
DFI = 2730
DFP = 2816
QK_SCALE = 8.0
EPS = 1e-5

TC = 512
NCH = T // TC
DC = DI // 128
CC = D // 128

FT = 512
NFT = T // FT
HALO = 264
NEG = -1.0e30

STAGE_LAYERS = 6
DEV_OUT = "x1d"
DEV_DEBUG = False
DEV_NCH = NCH
DEV_PHASES = (1, 2)

ENGS = ("sync", "tensor", "vector", "scalar", "gpsimd")


class Prog:
    def __init__(self, nc, ctx):
        self.nc = nc
        self.ctx = ctx
        self.q = {k: [] for k in ENGS}
        self.sems = {}
        self.cnt = {}

    def sem(self, name):
        h = self.ctx.enter_context(self.nc.semaphore(name))
        self.sems[name] = h
        self.cnt[name] = 0
        return h

    def emit(self, eng, make):
        self.q[eng].append(lambda e, mk=make: mk(e))

    def inc(self, eng, make, sem, n=1):
        s = self.sems[sem]
        def fn(e, mk=make, s=s, n=n):
            mk(e).then_inc(s, n)
        self.q[eng].append(fn)
        self.cnt[sem] += n
        return self.cnt[sem]

    def wait(self, eng, sem, val=None):
        v = self.cnt[sem] if val is None else val
        assert v <= self.cnt[sem], f"forward wait {sem}: {v} > {self.cnt[sem]}"
        s = self.sems[sem]
        def fn(e, s=s, v=v):
            e.wait_ge(s, v)
        self.q[eng].append(fn)

    def barrier(self, engines=ENGS):
        s = self.sems["s_bar"]
        for eng in engines:
            def fn(e, s=s):
                if hasattr(e, "engine_nop"):
                    e.engine_nop().then_inc(s, 1)
                else:
                    e.nop().then_inc(s, 1)
            self.q[eng].append(fn)
            self.cnt["s_bar"] += 1
        tgt = self.cnt["s_bar"]
        for eng in engines:
            self.wait(eng, "s_bar", tgt)


class Regions:

    def __init__(self, nc):
        self.nc = nc
        self.regions = {}
        self.order = []

    def region(self, name, size):
        self.regions[name] = [size, []]
        self.order.append(name)

    def place(self, region, name, shape, dtype, off=0):
        per = _reduce(lambda a, b: a * b, shape[1:], 1) * mybir.dt.size(dtype)
        r = self.regions[region]
        assert off + per <= r[0], \
            f"{name} ({off}+{per}) exceeds region {region} ({r[0]})"
        h = self.nc.virtual_sbuf_tensor(name, list(shape), dtype)
        r[1].append((h, off))
        return h

    def allocate(self, ctx):
        align = 64
        total = sum((r[0] + align - 1) // align * align
                    for r in self.regions.values())
        arena = ctx.enter_context(self.nc.sbuf_tensor(
            "arena", [128, total], mybir.dt.uint8))
        base = self.nc.lookup_mloc(arena).addr
        off = 0
        for name in self.order:
            size, items = self.regions[name]
            for h, toff in items:
                self.nc._tensor(h.name, h.shape, h.dtype, type="SB")
                mloc = self.nc.lookup_mloc(h)
                mloc.allocated = not self.nc.target_bir_lowering
                mloc.addr = base + off + toff
            off += (size + align - 1) // align * align


def nsl(w, cap=512):
    out = []
    o = 0
    while o < w:
        s = min(cap, w - o)
        out.append((o, s))
        o += s
    return out


KB = 1024


def build_program():
    nc = bass.Bass("TRN2", target_bir_lowering=False, debug=False)
    ctx = ExitStack()

    d = {}
    def din(name, shape, dt=F32):
        d[name] = nc.dram_tensor(name, list(shape), dt, kind="ExternalInput")
        return d[name]

    def dint(name, shape, dt=F32):
        d[name] = nc.dram_tensor(name, list(shape), dt, kind="Internal")
        return d[name]

    din("xs1", (D, 512 + T))
    din("xs2", (D, 512 + T))
    din("xnat", (D, T))
    for p in (1, 2):
        din(f"WinT{p}", (D, 2 * DI))
        din(f"Wconvd{p}", (128, DC * 4 * 128))
        din(f"WxT{p}", (DI, RNK + 2 * NST))
        din(f"WdtT{p}", (RNK, DI))
        din(f"WoutT{p}", (DI, D))
        din(f"Alog{p}", (128, DC * NST))
        din(f"bconv{p}", (128, DC))
        din(f"bdt{p}", (128, DC))
        din(f"Dp{p}", (128, DC))
    din("ln_w", (128, CC))
    din("ln_b", (128, CC))
    din("ones1", (1, 128))
    din("ones_c", (128, 1))
    din("cvals", (128, 16))
    din("onesb", (1, 128), BF16)
    for nm in ("m_id1", "m_rev1", "m_id2", "m_rev2", "m_lo", "m_hi"):
        din(nm, (128, 1))
    if STAGE_LAYERS >= 0:
        for li in range(DEPTH):
            din(f"WqkT{li}", (D, 2 * D))
            din(f"Wv{li}", (D, D))
            din(f"Wo{li}", (D, D))
            din(f"lng{li}", (128, CC))
            din(f"lnbf{li}", (128, CC))
            din(f"W1T{li}", (D, 2 * DFP))
            din(f"b1p{li}", (128, 2 * DFP // 128))
            din(f"W2T{li}", (DFP, D))
            din(f"b2{li}", (128, CC))
        din("rcq", (128, 512 + T)); din("rsq", (128, 512 + T))
        din("rck", (128, 512 + T)); din("rsk", (128, 512 + T))
        din("maskS", (NFT * 4 * 128, 3 * 128))
        din("maskP", (NFT * 4 * 3 * 128, 128), BF16)
        din("qsp", (128, DEPTH)); din("ksp", (128, DEPTH))
        din("blk2", (128, 2)); din("blk2b", (2, 128))
        din("blkA", (1, 128)); din("blkB", (1, 128))
        din("negb", (1, 128), BF16)
        din("onescb", (128, 1), BF16)
        din("idf32", (128, 128))
        din("rotm", (128, 128))

    out_t = nc.dram_tensor("out", [D, T], F32, kind="ExternalOutput")
    d["out"] = out_t
    if DEV_DEBUG:
        for nm, shape, dt in (("dbg_xn", (128, CC, TC), BF16),
                              ("dbg_xp", (128, DC, 4 + TC), BF16),
                              ("dbg_xc", (128, DC, TC), BF16),
                              ("dbg_dtl", (RNK, TC), BF16),
                              ("dbg_bc", (128, 8, TC), F32),
                              ("dbg_ys", (128, DC, TC), F32),
                              ("dbg_gated", (128, DC, TC), BF16),
                              ("dbg_tp", (128, 4, TC), F32),
                              ("dbg_h4", (128, TC, 4), F32),
                              ("dbg_pxn", (128, CC, 4), BF16),
                              ("dbg_pxp", (128, DC, 4), BF16),
                              ("dbg_pxb", (128, DC, 4), BF16),
                              ("dbg_q", (128, CC, FT), BF16),
                              ("dbg_k", (128, 2, CC, FT), BF16),
                              ("dbg_v", (128, 2, 4, H * DH), BF16),
                              ("dbg_o", (128, CC, FT), F32),
                              ("dbg_zs", (128, CC, FT), BF16),
                              ("dbg_pt", (128, 384), BF16),
                              ("dbg_den", (1, 2 * FT), F32)):
            d[nm] = nc.dram_tensor(nm, list(shape), dt, kind="ExternalOutput")

    dint("y1d", (D, T))
    dint("y2d", (D, T))
    dint("x1d", (D, T))
    dint("za", (D, T))
    dint("zb", (D, T))
    dint("bcd", (2, 32, TC))
    dint("cc_ci", (128, DC * NST))
    dint("cc_co", (256, DC * NST))
    if STAGE_LAYERS >= 0:
        dint("mrow", (1, 4 * H * 128), BF16)
        dint("cc_hi", (D, HALO))
        dint("cc_ho", (2 * D, HALO))
        dint("cc_ti", (D, 1))
        dint("cc_to", (2 * D, 1))

    pr = Prog(nc, ctx)
    for s in ("s_bar", "s_ld", "s_gd", "s_te", "s_dv", "s_ac", "s_gp", "s_cc",
              "s_st"):
        pr.sem(s)

    R = Regions(nc)
    R.region("RA", 16 * KB)
    R.region("RB", 16 * KB)
    R.region("RC", 16 * KB)
    R.region("RE", 16 * KB)
    R.region("RF", 8 * KB)
    R.region("RG", 32 * KB)
    R.region("RH", 516 * 2 * DC)
    R.region("RH2", 516 * 2 * DC)
    R.region("RI", 8 * KB)
    R.region("RJ", 16 * KB)
    R.region("RL", 8 * KB)
    R.region("RM", 2 * KB)
    R.region("RN", 25 * KB)

    sb = {}
    def place(reg, name, shape, dt=F32, off=0):
        sb[name] = R.place(reg, name, shape, dt, off=off)
        return sb[name]

    place("RA", "xin", (128, CC, TC))
    place("RB", "scr8", (128, CC, TC))
    place("RB", "gated", (128, DC, TC), BF16)
    place("RC", "xc_bf", (128, DC, TC), BF16)
    place("RC", "yout", (128, CC, TC))
    place("RE", "bc4", (128, 8, TC))
    place("RE", "yrev", (128, CC, TC))
    place("RF", "h4", (128, TC, 4))
    place("RG", "y_s", (128, DC, TC))
    place("RG", "rows", (1, 5 * TC))
    place("RH", "xp_a", (128, DC, 4 + TC), BF16)
    place("RH2", "xp_b", (128, DC, 4 + TC), BF16)
    place("RI", "xn_bf", (128, CC, TC), BF16)
    place("RJ", "wout_t", (128, 2, DC, 256), BF16)
    place("RJ", "win_t", (128, 2, CC, 256), BF16)
    place("RA", "ebt", (128, 4, TC))
    place("RL", "tp", (128, 4, TC))
    place("RM", "carrysel", (128, 2 * DC * NST))
    if STAGE_LAYERS >= 0:
        place("RA", "kbuf", (128, 2, CC, FT), BF16)
        place("RB", "o32", (128, CC, FT))
        place("RC", "scratch32", (128, CC, FT))
        place("RE", "wv_sb", (128, CC, D), BF16)
        place("RF", "mskS_sb", (128, 4, 384))
        place("RG", "mrowr", (1, 4 * H * 128), BF16)
        place("RG", "rtab", (128, 4, FT), off=16384)
        place("RG", "ag_bf", (128, DFP // 128, FT), BF16)
        place("RH", "zt", (128, CC, 516))
        place("RH2", "zstage", (128, CC, 516))
        place("RI", "zs_bf", (128, CC, FT), BF16)
        place("RI", "znf_bf", (128, CC, FT), BF16)
        place("RJ", "wqk_t", (128, 2, CC, 256), BF16)
        place("RJ", "wo_t", (128, 2, CC, 128), BF16, off=8192)
        place("RJ", "w1_t", (128, 2, CC, 256), BF16)
        place("RJ", "w2_t", (128, DFP // 128, 128), BF16, off=8192)
        place("RL", "mskP_sb", (128, 12, 128), BF16)
        place("RL", "den", (1, 2 * FT), off=4096)
        place("RN", "vbuf", (128, 2, 4, H * DH), BF16)
        place("RN", "q_bf", (128, CC, FT), BF16, off=16384)

    def Sreal(name, shape, dt=F32):
        sb[name] = ctx.enter_context(nc.sbuf_tensor(name, list(shape), dt))
        return sb[name]

    place("RN", "convd_t", (128, 2, 4, 128), BF16, off=0)
    place("RN", "wx_t", (128, DC, RNK + 2 * NST), BF16, off=2048)
    place("RN", "wdt_t", (RNK, DI), BF16, off=5120)
    place("RN", "winz_t", (128, 2, CC, 128), BF16, off=9216)
    place("RN", "dtlin", (RNK, TC), BF16, off=13312)
    place("RN", "browB", (1, TC), off=14336)
    place("RN", "browC", (1, TC), off=16384)
    place("RN", "Adev", (128, DC * NST), off=18432)
    Sreal("carry", (128, DC * NST))
    Sreal("bconv_t", (128, DC))
    Sreal("bdt_t", (128, DC))
    Sreal("D_t", (128, DC))
    Sreal("lnw_t", (128, CC))
    Sreal("lnb_t", (128, CC))
    Sreal("ones1_t", (1, 128))
    Sreal("onesc_t", (128, 1))
    Sreal("cvals_t", (128, 16))
    Sreal("onesb_t", (1, 128), BF16)
    for nm in ("mid1", "mrev1", "mid2", "mrev2", "mlo", "mhi"):
        Sreal(nm, (128, 1))
    if STAGE_LAYERS >= 0:
        Sreal("tok_sb", (128, CC, 2))
        Sreal("m_cols", (128, 64))
        Sreal("mrow_sb", (64, 128), BF16)
        Sreal("pt_bf", (128, 2, 384), BF16)
        Sreal("den2", (2, FT))
        Sreal("den2b", (2, FT))
        Sreal("qsp_t", (128, DEPTH)); Sreal("ksp_t", (128, DEPTH))
        Sreal("blk2_t", (128, 2)); Sreal("blk2b_t", (2, 128))
        Sreal("blkA_t", (1, 128)); Sreal("blkB_t", (1, 128))
        Sreal("negb_t", (1, 128), BF16)
        Sreal("onescb_t", (128, 1), BF16)
        Sreal("idf32_t", (128, 128))
        Sreal("rotm_t", (128, 128))
        Sreal("lng_t", (128, CC)); Sreal("lnbf_t", (128, CC))
        Sreal("b1p_t", (128, 2 * DFP // 128)); Sreal("b2_t", (128, CC))

    R.allocate(ctx)

    CVALS = [EPS, 1.0, -1.0, QK_SCALE, 0.5, 1.0 / D, 0.0, 2.0]
    for i, v in enumerate(CVALS):
        nc.const_aps.register(F32, v, sb["cvals_t"][:, i:i + 1])

    ps = {}
    def PS(name, shape, dt=F32):
        ps[name] = ctx.enter_context(nc.psum_tensor(name, list(shape), dt))
        return ps[name]

    PS("pA", (128, 512))
    PS("pB", (128, 512))
    PS("pS", (128, 512))
    PS("pS2", (128, 512))
    PS("pQ", (128, 512))
    PS("pR", (128, 512))
    PS("pV", (128, 512))
    PS("pW", (128, 512))

    def dma(eng, sem, out, in_):
        return pr.inc(eng, lambda e, o=out, i=in_: e.dma_start(out=o, in_=i),
                      sem, 16)

    fm = lambda t: t.ap().rearrange("(c p) t -> p c t", p=128)

    xs_by_phase = {1: d["xs1"], 2: d["xs2"]}

    def emit_load_consts():
        dma("sync", "s_ld", sb["ones1_t"][:, :], d["ones1"][:, :])
        dma("sync", "s_ld", sb["onesc_t"][:, :], d["ones_c"][:, :])
        dma("sync", "s_ld", sb["cvals_t"][:, :], d["cvals"][:, :])
        dma("sync", "s_ld", sb["onesb_t"][:, :], d["onesb"][:, :])
        dma("sync", "s_ld", sb["lnw_t"][:, :], d["ln_w"][:, :])
        dma("sync", "s_ld", sb["lnb_t"][:, :], d["ln_b"][:, :])
        for nm, src in (("mid1", "m_id1"), ("mrev1", "m_rev1"),
                        ("mid2", "m_id2"), ("mrev2", "m_rev2"),
                        ("mlo", "m_lo"), ("mhi", "m_hi")):
            dma("sync", "s_ld", sb[nm][:, :], d[src][:, :])
        if STAGE_LAYERS >= 0:
            for nm in ("qsp", "ksp", "blk2", "blk2b", "blkA", "blkB", "negb",
                       "onescb", "idf32", "rotm"):
                dma("sync", "s_ld", sb[nm + "_t"][:, :], d[nm][:, :])
        pr.wait("sync", "s_ld")
        pr.barrier()

    def emit_load_phase_weights(p):
        dma("gpsimd", "s_gd", sb["wx_t"][:, :, :],
            d[f"WxT{p}"].ap().rearrange("(k p) m -> p k m", p=128))
        dma("gpsimd", "s_gd", sb["wdt_t"][:, :], d[f"WdtT{p}"][:, :])
        dma("sync", "s_ld", sb["bconv_t"][:, :], d[f"bconv{p}"][:, :])
        dma("sync", "s_ld", sb["bdt_t"][:, :], d[f"bdt{p}"][:, :])
        dma("sync", "s_ld", sb["D_t"][:, :], d[f"Dp{p}"][:, :])
        dma("sync", "s_ld", sb["Adev"][:, :], d[f"Alog{p}"][:, :])
        pr.wait("scalar", "s_ld")
        pr.wait("scalar", "s_gd")
        a1 = pr.inc("scalar", lambda e: e.activation(
            out=sb["Adev"][:, :], in_=sb["Adev"][:, :], func=AF.Exp), "s_ac")
        pr.wait("vector", "s_ac", a1)
        pr.inc("vector", lambda e: e.tensor_scalar_mul(
            out=sb["Adev"][:, :], in0=sb["Adev"][:, :], scalar1=-1.0), "s_dv")
        pr.barrier()

    def emit_ln(src_ap, w, dst_bf=None, keep_x=False):
        xin, scr, rows = sb["xin"], sb["scr8"], sb["rows"]
        Rr = lambda i: rows[0:1, i * TC: i * TC + w]
        ld = dma("sync", "s_ld", xin[:, :, :w], src_ap)
        pr.wait("scalar", "s_ld", ld)
        sq = pr.inc("scalar", lambda e: e.activation(
            out=scr[:, :, :w], in_=xin[:, :, :w], func=AF.Square), "s_ac")
        pr.wait("tensor", "s_ld", ld)
        pr.wait("tensor", "s_ac", sq)
        def mk_stats(e, w=w):
            last = None
            for src, pp in ((xin, ps["pS"]), (scr, ps["pS2"])):
                for c in range(CC):
                    last = e.matmul(out=pp[0:1, :w], lhsT=sb["onesc_t"][:, :],
                                    rhs=src[:, c, :w],
                                    start=(c == 0), stop=(c == CC - 1))
            return last
        st = pr.inc("tensor", mk_stats, "s_te")
        pr.wait("vector", "s_te", st)
        pr.inc("vector", lambda e: e.tensor_scalar_mul(
            out=Rr(0), in0=ps["pS"][0:1, :w], scalar1=1.0 / D), "s_dv")
        pr.inc("vector", lambda e: e.tensor_scalar_mul(
            out=Rr(1), in0=ps["pS2"][0:1, :w], scalar1=1.0 / D), "s_dv")
        pr.inc("vector", lambda e: e.tensor_mul(out=Rr(2), in0=Rr(0), in1=Rr(0)), "s_dv")
        dv = pr.inc("vector", lambda e: e.tensor_sub(out=Rr(1), in0=Rr(1), in1=Rr(2)), "s_dv")
        pr.wait("scalar", "s_dv", dv)
        ac = pr.inc("scalar", lambda e: e.activation(
            out=Rr(2), in_=Rr(1), func=AF.Sqrt, bias=EPS), "s_ac")
        pr.wait("vector", "s_ac", ac)
        dv = pr.inc("vector", lambda e: e.reciprocal(out=Rr(3), in_=Rr(2)), "s_dv")
        if dst_bf is not None:
            pr.inc("vector", lambda e: e.tensor_mul(out=Rr(4), in0=Rr(1), in1=Rr(3)), "s_dv")
            dv = pr.inc("vector", lambda e: e.tensor_mul(out=Rr(4), in0=Rr(4), in1=Rr(3)), "s_dv")
            pr.wait("scalar", "s_dv", dv)
            ac = pr.inc("scalar", lambda e: e.activation(
                out=Rr(2), in_=Rr(4), func=AF.Sqrt, bias=EPS), "s_ac")
            pr.wait("vector", "s_ac", ac)
            pr.inc("vector", lambda e: e.reciprocal(out=Rr(4), in_=Rr(2)), "s_dv")
            dv = pr.inc("vector", lambda e: e.tensor_mul(out=Rr(3), in0=Rr(3), in1=Rr(4)), "s_dv")
        pr.wait("tensor", "s_dv", dv)
        def mk_bc(e, w=w):
            e.matmul(out=ps["pS"][:, :w], lhsT=sb["ones1_t"][:, :], rhs=Rr(0),
                     start=True, stop=True)
            return e.matmul(out=ps["pS2"][:, :w], lhsT=sb["ones1_t"][:, :],
                            rhs=Rr(3), start=True, stop=True)
        bc = pr.inc("tensor", mk_bc, "s_te")
        pr.wait("vector", "s_te", bc)
        for c in range(CC):
            pr.inc("vector", lambda e, c=c: e.tensor_sub(
                out=scr[:, c, :w], in0=xin[:, c, :w], in1=ps["pS"][:, :w]), "s_dv")
            if dst_bf is not None:
                pr.inc("vector", lambda e, c=c: e.tensor_mul(
                    out=scr[:, c, :w], in0=scr[:, c, :w], in1=ps["pS2"][:, :w]),
                    "s_dv")
                dv = pr.inc("vector", lambda e, c=c: e.tensor_scalar(
                    out=dst_bf[:, c, :w], in0=scr[:, c, :w],
                    scalar1=sb["lnw_t"][:, c:c + 1],
                    scalar2=sb["lnb_t"][:, c:c + 1],
                    op0=OP.mult, op1=OP.add), "s_dv")
            else:
                dv = pr.inc("vector", lambda e, c=c: e.tensor_mul(
                    out=scr[:, c, :w], in0=scr[:, c, :w], in1=ps["pS2"][:, :w]),
                    "s_dv")
        return dv

    def emit_win_xp(p, w, dst_off):
        te_done = {}
        done = {}
        for mb in range(8):
            buf = mb % 2
            if mb >= 2:
                pr.wait("gpsimd", "s_te", te_done[mb * 2 - 3])
            gd = dma("gpsimd", "s_gd", sb["win_t"][:, buf, :, :],
                     d[f"WinT{p}"].ap().rearrange("(k p) m -> p k m", p=128)
                     [:, :, mb * 256:(mb + 1) * 256])
            pr.wait("tensor", "s_gd", gd)
            for m01 in (0, 1):
                dc = mb * 2 + m01
                pp = ps["pA"] if dc % 2 == 0 else ps["pB"]
                if dc >= 2:
                    pr.wait("tensor", done[dc - 2][0], done[dc - 2][1])
                def mk(e, buf=buf, m01=m01, pp=pp, w=w):
                    last = None
                    for k in range(CC):
                        last = e.matmul(
                            out=pp[:, :w],
                            lhsT=sb["win_t"][:, buf, k, m01 * 128:(m01 + 1) * 128],
                            rhs=sb["xn_bf"][:, k, :w],
                            start=(k == 0), stop=(k == CC - 1))
                    return last
                te_done[dc] = pr.inc("tensor", mk, "s_te")
                if dst_off is None:
                    pr.wait("scalar", "s_te", te_done[dc])
                    pr.inc("scalar", lambda e, dc=dc, pp=pp: e.copy(
                        out=sb["xp_a"][:, dc, 0:4], in_=pp[:, 508:512]), "s_ac")
                    ac = pr.inc("scalar", lambda e, dc=dc, pp=pp: e.copy(
                        out=sb["xp_b"][:, dc, 0:3], in_=pp[:, 509:512]), "s_ac")
                    done[dc] = ("s_ac", ac)
                else:
                    pr.wait("scalar", "s_te", te_done[dc])
                    ac = pr.inc("scalar", lambda e, dc=dc, pp=pp, w=w: e.copy(
                        out=sb["xp_a"][:, dc, 4:4 + w], in_=pp[:, :w]), "s_ac")
                    pr.wait("vector", "s_ac", ac)
                    pr.inc("vector", lambda e, dc=dc, w=w: e.tensor_copy(
                        out=sb["xp_b"][:, dc, 3:3 + w],
                        in_=sb["xp_a"][:, dc, 4:4 + w]), "s_dv")
                    done[dc] = ("s_ac", ac)
        pr.barrier(("tensor", "scalar", "gpsimd", "vector"))

    def emit_conv(p, dc, w, out_ap, pp):
        buf = dc % 2
        pr.wait("gpsimd", "s_te")
        gd = dma("gpsimd", "s_gd", sb["convd_t"][:, buf, :, :],
                 d[f"Wconvd{p}"].ap().rearrange("p (dc k m) -> p dc k m",
                                                dc=DC, k=4)[:, dc, :, :])
        pr.wait("tensor", "s_gd", gd)
        def mk(e, dc=dc, w=w, pp=pp, buf=buf):
            srcs = ((sb["xp_b"], 0), (sb["xp_a"], 2), (sb["xp_b"], 2),
                    (sb["xp_a"], 4))
            last = None
            for k in range(4):
                tt, o0 = srcs[k]
                last = e.matmul(out=pp[:, :w],
                                lhsT=sb["convd_t"][:, buf, k, :],
                                rhs=tt[:, dc, o0:o0 + w],
                                start=(k == 0), stop=(k == 3))
            return last
        te = pr.inc("tensor", mk, "s_te")
        pr.wait("scalar", "s_te", te)
        ac = pr.inc("scalar", lambda e, dc=dc, pp=pp, w=w: e.activation(
            out=out_ap, in_=pp[:, :w], func=AF.Silu,
            bias=sb["bconv_t"][:, dc:dc + 1]), "s_ac")
        return ac

    def emit_mamba_chunk(p, c, xs, ydst):
        w = TC
        off = 512 + c * TC
        if c > 0:
            pr.inc("vector", lambda e: e.tensor_copy(
                out=sb["xp_a"][:, :, 0:4], in_=sb["xp_a"][:, :, TC:TC + 4]),
                "s_dv")
            pr.inc("vector", lambda e: e.tensor_copy(
                out=sb["xp_b"][:, :, 0:3], in_=sb["xp_b"][:, :, TC:TC + 3]),
                "s_dv")
        pr.barrier()
        emit_ln(fm(xs)[:, :, off:off + w], w, dst_bf=sb["xn_bf"])
        pr.barrier()
        emit_win_xp(p, w, 4)
        for dc in range(DC):
            pp = ps["pA"] if dc % 2 == 0 else ps["pB"]
            ac = emit_conv(p, dc, w, sb["xc_bf"][:, dc, :w], pp)
            pr.wait("tensor", "s_ac", ac)
        pr.barrier(("tensor", "scalar", "gpsimd", "vector", "sync"))
        def mk_xdbl(e, w=w):
            last = None
            for k in range(DC):
                last = e.matmul(out=ps["pS"][0:96, :w],
                                lhsT=sb["wx_t"][:, k, :],
                                rhs=sb["xc_bf"][:, k, :w],
                                start=(k == 0), stop=(k == DC - 1))
            return last
        te = pr.inc("tensor", mk_xdbl, "s_te")
        pr.wait("scalar", "s_te", te)
        pr.inc("scalar", lambda e, w=w: e.copy(
            out=sb["dtlin"][:, :w], in_=ps["pS"][0:64, :w]), "s_ac")
        a2 = pr.inc("scalar", lambda e, w=w: e.copy(
            out=sb["tp"][0:32, 0, :w], in_=ps["pS"][64:96, :w]), "s_ac")
        pr.wait("sync", "s_ac", a2)
        bcw = dma("sync", "s_ld", d["bcd"].ap()[c % 2, :, :w], sb["tp"][0:32, 0, :w])
        pr.barrier()

        for nb in range(4):
            pr.wait("sync", "s_ld", bcw)
            for n8 in range(4):
                n = nb * 4 + n8
                l1 = dma("sync", "s_ld", sb["browB"][0:1, :w],
                         d["bcd"].ap()[c % 2, n:n + 1, :w])
                l2 = dma("sync", "s_ld", sb["browC"][0:1, :w],
                         d["bcd"].ap()[c % 2, NST + n:NST + n + 1, :w])
                pr.wait("tensor", "s_ld", l2)
                def mkb(e, w=w):
                    e.matmul(out=ps["pR"][:, :w], lhsT=sb["ones1_t"][:, :],
                             rhs=sb["browB"][0:1, :w], start=True, stop=True)
                    return e.matmul(out=ps["pQ"][:, :w], lhsT=sb["ones1_t"][:, :],
                                    rhs=sb["browC"][0:1, :w], start=True, stop=True)
                te = pr.inc("tensor", mkb, "s_te")
                pr.wait("scalar", "s_te", te)
                pr.inc("scalar", lambda e, n8=n8, w=w: e.copy(
                    out=sb["bc4"][:, n8, :w], in_=ps["pR"][:, :w]), "s_ac")
                ac = pr.inc("scalar", lambda e, n8=n8, w=w: e.copy(
                    out=sb["bc4"][:, 4 + n8, :w], in_=ps["pQ"][:, :w]), "s_ac")
                pr.wait("tensor", "s_ac", ac)
            pr.barrier(("tensor", "scalar", "vector", "gpsimd", "sync"))
            for dc in range(DC):
                emit_scan_dc(p, c, nb, dc)
            pr.barrier(("tensor", "scalar", "vector", "gpsimd"))
        if DEV_DEBUG and p == 1 and c == 0:
            for nm, t in (("dbg_xn", "xn_bf"), ("dbg_xp", "xp_a"),
                          ("dbg_xc", "xc_bf"), ("dbg_dtl", "dtlin"),
                          ("dbg_bc", "bc4"), ("dbg_ys", "y_s"),
                          ("dbg_gated", "gated"), ("dbg_tp", "tp"),
                          ("dbg_h4", "h4")):
                dma("sync", "s_st", d[nm].ap(), sb[t].ap())
            pr.wait("sync", "s_st")
            pr.barrier()
        emit_wout(p, c, ydst)
        pr.barrier()

    def emit_scan_dc(p, c, nb, dc):
        w = TC
        pr.wait("scalar", "s_dv")
        pr.wait("gpsimd", "s_dv")
        pr.wait("tensor", "s_dv")
        pr.wait("tensor", "s_ac")
        tp = sb["tp"]
        t_dt = tp[:, 0, :]
        t_a = tp[:, 1, :]
        t_b = tp[:, 2, :]
        t_c = tp[:, 3, :]
        def mkdt(e, dc=dc, w=w):
            return e.matmul(out=ps["pQ"][:, :w],
                            lhsT=sb["wdt_t"][:, dc * 128:(dc + 1) * 128],
                            rhs=sb["dtlin"][:, :w], start=True, stop=True)
        te = pr.inc("tensor", mkdt, "s_te")
        pr.wait("scalar", "s_te", te)
        pr.inc("scalar", lambda e, dc=dc: e.activation(
            out=t_a, in_=ps["pQ"][:, :w], func=AF.Relu,
            bias=sb["bdt_t"][:, dc:dc + 1]), "s_ac")
        ac = pr.inc("scalar", lambda e, dc=dc: e.activation(
            out=t_b, in_=ps["pQ"][:, :w], func=AF.Abs,
            bias=sb["bdt_t"][:, dc:dc + 1]), "s_ac")
        pr.wait("tensor", "s_ac", ac)
        pr.inc("scalar", lambda e: e.activation(
            out=t_b, in_=t_b, func=AF.Exp, scale=-1.0), "s_ac")
        ac = pr.inc("scalar", lambda e: e.activation(
            out=t_b, in_=t_b, func=AF.Ln, bias=1.0), "s_ac")
        pr.wait("vector", "s_ac", ac)
        dv_dt = pr.inc("vector", lambda e: e.tensor_add(
            out=t_dt, in0=t_a, in1=t_b), "s_dv")
        ac_xc = emit_conv(p, dc, w, t_a, ps["pQ"])
        pr.wait("vector", "s_ac", ac_xc)
        dv_w = pr.inc("vector", lambda e: e.tensor_mul(
            out=t_b, in0=t_dt, in1=t_a), "s_dv")
        pr.wait("gpsimd", "s_dv", dv_w)
        pr.wait("scalar", "s_dv", dv_dt)
        dvS = {}
        for n8 in range(4):
            n = nb * 4 + n8
            eb = n8 % 2
            if n8 >= 2:
                pr.wait("scalar", "s_dv", dvS[n8 - 2])
                pr.wait("gpsimd", "s_dv", dvS[n8 - 2])
            acE = pr.inc("scalar", lambda e, dc=dc, n=n, eb=eb: e.activation(
                out=sb["ebt"][:, eb, :], in_=t_dt, func=AF.Exp,
                scale=sb["Adev"][:, dc * NST + n: dc * NST + n + 1]), "s_ac")
            gpB = pr.inc("gpsimd", lambda e, n8=n8, eb=eb: e.tensor_mul(
                out=sb["ebt"][:, 2 + eb, :], in0=t_b,
                in1=sb["bc4"][:, n8, :]), "s_gp")
            pr.wait("vector", "s_ac", acE)
            pr.wait("vector", "s_gp", gpB)
            dvS[n8] = pr.inc("vector",
                             lambda e, dc=dc, n=n, n8=n8, eb=eb: e.tensor_tensor_scan(
                out=sb["h4"][:, :, n8], data0=sb["ebt"][:, eb, :],
                data1=sb["ebt"][:, 2 + eb, :],
                initial=sb["carry"][:, dc * NST + n: dc * NST + n + 1],
                op0=OP.mult, op1=OP.add), "s_dv")
        pr.inc("vector", lambda e, dc=dc, nb=nb: e.tensor_copy(
            out=sb["carry"][:, dc * NST + nb * 4: dc * NST + nb * 4 + 4],
            in_=sb["h4"][:, TC - 1, :]), "s_dv")
        for n8 in range(4):
            pr.inc("vector", lambda e, n8=n8: e.tensor_mul(
                out=sb["h4"][:, :, n8], in0=sb["h4"][:, :, n8],
                in1=sb["bc4"][:, 4 + n8, :]), "s_dv")
        if nb == 0:
            pr.inc("vector", lambda e, dc=dc: e.reduce_sum(
                out=sb["y_s"][:, dc, :], in_=sb["h4"][:, :, :], axis=AX.X), "s_dv")
        else:
            pr.inc("vector", lambda e: e.reduce_sum(
                out=t_c, in_=sb["h4"][:, :, :], axis=AX.X), "s_dv")
            dv_ya = pr.inc("vector", lambda e, dc=dc: e.tensor_add(
                out=sb["y_s"][:, dc, :], in0=sb["y_s"][:, dc, :], in1=t_c), "s_dv")
        if nb == 3:
            buf = dc % 2
            pr.wait("gpsimd", "s_te")
            gd = dma("gpsimd", "s_gd", sb["winz_t"][:, buf, :, :],
                     d[f"WinT{p}"].ap().rearrange("(k p) m -> p k m", p=128)
                     [:, :, DI + dc * 128: DI + (dc + 1) * 128])
            pr.wait("tensor", "s_gd", gd)
            def mkz(e, buf=buf, w=w):
                last = None
                for k in range(CC):
                    last = e.matmul(out=ps["pR"][:, :w],
                                    lhsT=sb["winz_t"][:, buf, k, :],
                                    rhs=sb["xn_bf"][:, k, :w],
                                    start=(k == 0), stop=(k == CC - 1))
                return last
            tez = pr.inc("tensor", mkz, "s_te")
            pr.wait("scalar", "s_te", tez)
            pr.wait("scalar", "s_dv", dv_ya)
            acz = pr.inc("scalar", lambda e, w=w: e.activation(
                out=t_c, in_=ps["pR"][:, :w], func=AF.Silu), "s_ac")
            pr.wait("vector", "s_ac", acz)
            pr.inc("vector", lambda e, dc=dc: e.scalar_tensor_tensor(
                out=t_b, in0=t_a, scalar=sb["D_t"][:, dc:dc + 1],
                in1=sb["y_s"][:, dc, :], op0=OP.mult, op1=OP.add), "s_dv")
            dvg = pr.inc("vector", lambda e, dc=dc, w=w: e.tensor_mul(
                out=sb["gated"][:, dc, :w], in0=t_b, in1=t_c), "s_dv")
            pr.wait("tensor", "s_dv", dvg)
            pr.wait("scalar", "s_dv", dvg)
        del dvS

    def emit_wout(p, c, ydst):
        w = TC
        ac_prev = {}
        for mb in range(4):
            buf = mb % 2
            if mb >= 2:
                pr.wait("gpsimd", "s_ac", ac_prev[mb * 2 - 3])
            gd = dma("gpsimd", "s_gd", sb["wout_t"][:, buf, :, :],
                     d[f"WoutT{p}"].ap().rearrange("(k p) m -> p k m", p=128)
                     [:, :, mb * 256:(mb + 1) * 256])
            pr.wait("tensor", "s_gd", gd)
            for m01 in (0, 1):
                m = mb * 2 + m01
                pp = ps["pA"] if m % 2 == 0 else ps["pB"]
                if m >= 2:
                    pr.wait("tensor", "s_ac", ac_prev[m - 2])
                def mk(e, buf=buf, m01=m01, pp=pp, w=w):
                    last = None
                    for k in range(DC):
                        last = e.matmul(
                            out=pp[:, :w],
                            lhsT=sb["wout_t"][:, buf, k, m01 * 128:(m01 + 1) * 128],
                            rhs=sb["gated"][:, k, :w],
                            start=(k == 0), stop=(k == DC - 1))
                    return last
                te = pr.inc("tensor", mk, "s_te")
                pr.wait("scalar", "s_te", te)
                ac_prev[m] = pr.inc("scalar", lambda e, m=m, pp=pp, w=w: e.copy(
                    out=sb["yout"][:, m, :w], in_=pp[:, :w]), "s_ac")
                pr.wait("sync", "s_ac", ac_prev[m])
                dma("sync", "s_st", fm(ydst)[:, m, c * TC:(c + 1) * TC],
                    sb["yout"][:, m, :w])
        pr.wait("sync", "s_st")

    def emit_mamba_phase(p):
        xs = xs_by_phase[p]
        ydst = d["y1d"] if p == 1 else d["y2d"]
        emit_ln(fm(xs)[:, :, 0:512], 512, dst_bf=sb["xn_bf"])
        pr.barrier()
        emit_win_xp(p, 512, None)
        pr.barrier()
        if DEV_DEBUG and p == 1:
            dma("sync", "s_st", d["dbg_pxp"].ap(), sb["xp_a"][:, :, 0:4])
            dma("sync", "s_st", d["dbg_pxb"].ap(), sb["xp_b"][:, :, 0:4])
            pr.wait("sync", "s_st"); pr.barrier()
        if p == 1:
            pr.inc("gpsimd", lambda e: e.memset(sb["carry"][:, :], 0.0), "s_gp")
            pr.barrier(("gpsimd", "vector"))
        for c in range(DEV_NCH):
            emit_mamba_chunk(p, c, xs, ydst)

    def emit_carry_exchange():
        st = dma("sync", "s_st", d["cc_ci"][:, :], sb["carry"][:, :])
        pr.wait("gpsimd", "s_st", st)
        cc = pr.inc("gpsimd", lambda e: e.collective_compute(
            "AllGather", OP.bypass,
            replica_groups=[[0, 1], [2, 3], [4, 5], [6, 7]],
            ins=[d["cc_ci"].ap()], outs=[d["cc_co"].ap()]), "s_cc", 1)
        pr.wait("gpsimd", "s_cc", cc)
        ld = dma("gpsimd", "s_gd",
                 sb["carrysel"][:, :].rearrange("p (r k) -> p r k", r=2),
                 d["cc_co"].ap().rearrange("(r p) k -> p r k", p=128))
        pr.wait("vector", "s_gd", ld)
        pr.inc("vector", lambda e: e.tensor_scalar_mul(
            out=sb["carry"][:, :], in0=sb["carrysel"][:, 0:DC * NST],
            scalar1=sb["mlo"][:, :]), "s_dv")
        pr.inc("vector", lambda e: e.scalar_tensor_tensor(
            out=sb["carry"][:, :], in0=sb["carrysel"][:, DC * NST:],
            scalar=sb["mhi"][:, :], in1=sb["carry"][:, :],
            op0=OP.mult, op1=OP.add), "s_dv")
        pr.barrier()

    def emit_combine():
        for c in range(NCH):
            w = TC
            emit_ln(fm(d["xnat"])[:, :, c * TC:(c + 1) * TC], w, dst_bf=None)
            for (ynm, mid, mrev) in (("y1d", "mid1", "mrev1"),
                                     ("y2d", "mid2", "mrev2")):
                rc = NCH - 1 - c
                pr.wait("sync", "s_dv")
                l1 = dma("sync", "s_ld", sb["yout"][:, :, :w],
                         fm(d[ynm])[:, :, c * TC:(c + 1) * TC])
                l2 = dma("sync", "s_ld", sb["yrev"][:, :, :w],
                         fm(d[ynm])[:, :, rc * TC:(rc + 1) * TC])
                pr.wait("vector", "s_ld", l2)
                for cc_ in range(CC):
                    pr.inc("vector", lambda e, cc_=cc_, mid=mid, w=w: e.tensor_scalar_mul(
                        out=sb["yout"][:, cc_, :w], in0=sb["yout"][:, cc_, :w],
                        scalar1=sb[mid][:, :]), "s_dv")
                    pr.inc("vector", lambda e, cc_=cc_, mrev=mrev, w=w: e.scalar_tensor_tensor(
                        out=sb["yout"][:, cc_, :w],
                        in0=sb["yrev"][:, cc_, :w][:, ::-1],
                        scalar=sb[mrev][:, :], in1=sb["yout"][:, cc_, :w],
                        op0=OP.mult, op1=OP.add), "s_dv")
                    pr.inc("vector", lambda e, cc_=cc_, w=w: e.scalar_tensor_tensor(
                        out=sb["xin"][:, cc_, :w], in0=sb["yout"][:, cc_, :w],
                        scalar=0.5, in1=sb["xin"][:, cc_, :w],
                        op0=OP.mult, op1=OP.add), "s_dv")
            for cc_ in range(CC):
                dv = pr.inc("vector", lambda e, cc_=cc_, w=w: e.tensor_add(
                    out=sb["xin"][:, cc_, :w], in0=sb["xin"][:, cc_, :w],
                    in1=sb["scr8"][:, cc_, :w]), "s_dv")
                pr.wait("sync", "s_dv", dv)
                dma("sync", "s_st", fm(d["x1d"])[:, cc_, c * TC:(c + 1) * TC],
                    sb["xin"][:, cc_, :w])
            pr.wait("sync", "s_st")
            pr.barrier()
        if STAGE_LAYERS >= 0:
            for c in range(NCH):
                w = TC
                emit_ln(fm(d["x1d"])[:, :, c * TC:(c + 1) * TC], w, dst_bf=None)
                for cc_ in range(CC):
                    pr.wait("sync", "s_dv")
                    dma("sync", "s_st", fm(d["za"])[:, cc_, c * TC:(c + 1) * TC],
                        sb["scr8"][:, cc_, :w])
                pr.wait("sync", "s_st")
                pr.barrier()


    TLORD = [2, 3, 4, 5, 6, 7, 0, 1]

    def emit_halo_exchange(zsrc, li):
        pr.wait("sync", "s_st")
        st = dma("sync", "s_st", d["cc_hi"].ap(), zsrc.ap()[:, T - HALO:T])
        pr.wait("gpsimd", "s_st", st)
        cc = pr.inc("gpsimd", lambda e: e.collective_compute(
            "AllGather", OP.bypass,
            replica_groups=[[0, 1], [2, 3], [4, 5], [6, 7]],
            ins=[d["cc_hi"].ap()], outs=[d["cc_ho"].ap()]), "s_cc", 1)
        pr.wait("gpsimd", "s_cc", cc)
        pr.inc("gpsimd", lambda e: e.memset(sb["zstage"][:, :, :], 0.0), "s_gp")
        pr.wait("gpsimd", "s_gp")
        ld = dma("gpsimd", "s_gd", sb["zstage"][:, :, 516 - HALO:516],
                 d["cc_ho"].ap()[0:D, :].rearrange("(c p) t -> p c t", p=128))
        pr.wait("vector", "s_gd", ld)
        for c in range(CC):
            pr.inc("vector", lambda e, c=c: e.tensor_scalar_mul(
                out=sb["zstage"][:, c, 516 - HALO:516],
                in0=sb["zstage"][:, c, 516 - HALO:516],
                scalar1=sb["mlo"][:, :]), "s_dv")
        pr.barrier()

    def emit_tok_exchange(zsrc):
        pr.wait("sync", "s_st")
        st = pr.inc("sync", lambda e: e.dma_start(
            out=d["cc_ti"].ap(), in_=zsrc.ap()[:, T - 1:T],
            allow_slow_non_contiguous=True), "s_st", 16)
        pr.wait("gpsimd", "s_st", st)
        cc = pr.inc("gpsimd", lambda e: e.collective_compute(
            "AllGather", OP.bypass,
            replica_groups=[[0, 1], [2, 3], [4, 5], [6, 7]],
            ins=[d["cc_ti"].ap()], outs=[d["cc_to"].ap()]), "s_cc", 1)
        pr.wait("gpsimd", "s_cc", cc)
        ld = pr.inc("gpsimd", lambda e: e.dma_start(
            out=sb["tok_sb"][:, :, 0:1],
            in_=d["cc_to"].ap()[0:D, :].rearrange("(c p) t -> p c t", p=128),
            allow_slow_non_contiguous=True), "s_gd", 16)
        pr.wait("vector", "s_gd", ld)
        for c in range(CC):
            pr.inc("vector", lambda e, c=c: e.tensor_scalar_mul(
                out=sb["tok_sb"][:, c, 0:1], in0=sb["tok_sb"][:, c, 0:1],
                scalar1=sb["mlo"][:, :]), "s_dv")
        pr.barrier()

    def shifted(src_tile, c, t0, w):
        if c < 4:
            return src_tile[:, c, t0:t0 + w]
        return src_tile[:, c, t0 - 1:t0 - 1 + w]

    def emit_qkv(li, src_tile, t0, goff, cur, want_q):
        s32 = sb["scratch32"]
        pr.wait("vector", "s_te")
        for c in range(CC):
            pr.inc("vector", lambda e, c=c: e.tensor_copy(
                out=sb["zs_bf"][:, c, :], in_=shifted(src_tile, c, t0, FT)),
                "s_dv")
        pr.wait("sync", "s_dv")
        pr.wait("sync", "s_gp")
        for i, tab in enumerate(("rcq", "rsq", "rck", "rsk")):
            dma("sync", "s_ld", sb["rtab"][:, i, :],
                d[tab].ap()[:, 512 + goff:512 + goff + FT])
        pr.wait("sync", "s_ld")
        pr.barrier(("vector", "tensor", "scalar", "gpsimd", "sync"))

        chunks = []
        for qk in range(2):
            if qk == 0 and not want_q:
                continue
            for c in range(CC):
                chunks.append((qk, c))
        EV = {}; SQ = {}; V1 = {}; U1 = {}; U2A = {}; STT = {}
        NM = {}; RT = {}; RB = {}; SD = {}; RC = {}; MMd = {}
        DEN = (sb["den2"], sb["den2b"])
        NRM = (0, 64)
        wq = [{}, {}]
        for idx, (qk, c) in enumerate(chunks):
            r = idx % 2
            sl = 3 * r
            pp = ps["pA"] if r == 0 else ps["pB"]
            mb, m01 = c // 2, c % 2
            if m01 == 0:
                buf = mb % 2
                if idx >= 3:
                    pr.wait("gpsimd", "s_te", MMd[idx - 3])
                gd = dma("gpsimd", "s_gd", sb["wqk_t"][:, buf, :, :],
                         d[f"WqkT{li}"].ap().rearrange("(k p) m -> p k m", p=128)
                         [:, :, qk * D + mb * 256: qk * D + (mb + 1) * 256])
                wq[qk if want_q else 1][mb] = gd
            buf = mb % 2
            pr.wait("tensor", "s_gd", wq[qk if want_q else 1][mb])
            if idx >= 2:
                pr.wait("tensor", "s_ac", EV[idx - 2])
            def mk(e, buf=buf, m01=m01, pp=pp):
                last = None
                for k in range(CC):
                    last = e.matmul(
                        out=pp[:, :FT],
                        lhsT=sb["wqk_t"][:, buf, k, m01 * 128:(m01 + 1) * 128],
                        rhs=sb["zs_bf"][:, k, :],
                        start=(k == 0), stop=(k == CC - 1))
                return last
            MM = pr.inc("tensor", mk, "s_te")
            MMd[idx] = MM
            if idx >= 1:
                emit_qkv_te_tail(idx - 1)
            pr.wait("scalar", "s_te", MM)
            if idx >= 2:
                pr.wait("scalar", "s_dv", STT[idx - 2])
            EV[idx] = pr.inc("scalar", lambda e, pp=pp, sl=sl: e.copy(
                out=s32[:, sl, :], in_=pp[:, :FT]), "s_ac")
            SQ[idx] = pr.inc("scalar", lambda e, sl=sl: e.activation(
                out=s32[:, sl + 1, :], in_=s32[:, sl, :], func=AF.Square), "s_ac")

            def emit_qkv_te_tail(i2, chunks=chunks):
                qk2, c2 = chunks[i2]
                r2 = i2 % 2
                sl2 = 3 * r2
                nb2 = NRM[r2]
                pr.wait("tensor", "s_ac", SQ[i2])
                if i2 >= 2:
                    pr.wait("tensor", "s_ac", SD[i2 - 2])
                NM[i2] = pr.inc("tensor", lambda e, sl2=sl2, nb2=nb2: e.matmul(
                    out=ps["pR"][nb2:nb2 + 2, :FT], lhsT=sb["blk2_t"][:, :],
                    rhs=s32[:, sl2 + 1, :], start=True, stop=True), "s_te")
                pr.wait("tensor", "s_dv", V1[i2])
                prot = ps["pS"] if r2 == 0 else ps["pS2"]
                if i2 >= 2:
                    pr.wait("tensor", "s_dv", U2A[i2 - 2])
                RT[i2] = pr.inc("tensor", lambda e, sl2=sl2, prot=prot: e.matmul(
                    out=prot[:, :FT], lhsT=sb["rotm_t"][:, :],
                    rhs=s32[:, sl2 + 2, :], start=True, stop=True), "s_te")

            def te_rbc(i2, chunks=chunks):
                r2 = i2 % 2
                pr.wait("tensor", "s_dv", RC[i2])
                if i2 >= 2:
                    pr.wait("tensor", "s_dv", STT[i2 - 2])
                pq = ps["pQ"] if r2 == 0 else ps["pV"]
                dt2 = DEN[r2]
                RB[i2] = pr.inc("tensor", lambda e, dt2=dt2, pq=pq: e.matmul(
                    out=pq[:, :FT], lhsT=sb["blk2b_t"][:, :],
                    rhs=dt2[0:2, :], start=True, stop=True), "s_te")

            def act_tail(i2):
                qk2, c2 = chunks[i2]
                r2 = i2 % 2
                nb2 = NRM[r2]
                dt2 = DEN[r2]
                pr.wait("scalar", "s_te", NM[i2])
                if i2 >= 2:
                    pr.wait("scalar", "s_te", RB[i2 - 2])
                SD[i2] = pr.inc("scalar", lambda e, nb2=nb2, dt2=dt2: e.activation(
                    out=dt2[0:2, :], in_=ps["pR"][nb2:nb2 + 2, :FT],
                    func=AF.Sqrt), "s_ac")

            sp = "qsp_t" if qk == 0 else "ksp_t"
            pr.wait("vector", "s_ac", EV[idx])
            if idx >= 2:
                pr.wait("vector", "s_te", RT[idx - 2])
            V1[idx] = pr.inc("vector", lambda e, sl=sl, sp=sp: e.tensor_scalar_mul(
                out=s32[:, sl + 2, :], in0=s32[:, sl, :],
                scalar1=sb[sp][:, li:li + 1]), "s_dv")
            if idx >= 1:
                act_tail(idx - 1)
                dve_head(idx - 1)
                te_rbc(idx - 1)
                dve_fin(idx - 1)

            def dve_head(i2):
                qk2, c2 = chunks[i2]
                r2 = i2 % 2
                sl2 = 3 * r2
                ct2 = 0 if qk2 == 0 else 2
                pr.wait("gpsimd", "s_te", NM[i2])
                pr.wait("gpsimd", "s_dv", V1[i2])
                U1[i2] = pr.inc("gpsimd", lambda e, sl2=sl2, ct2=ct2: e.tensor_mul(
                    out=s32[:, sl2 + 1, :], in0=s32[:, sl2 + 2, :],
                    in1=sb["rtab"][:, ct2, :]), "s_gp")
                dt2 = DEN[r2]
                pr.wait("vector", "s_ac", SD[i2])
                pr.inc("vector", lambda e, dt2=dt2: e.tensor_scalar_max(
                    out=dt2[0:2, :], in0=dt2[0:2, :], scalar1=1e-12), "s_dv")
                RC[i2] = pr.inc("vector", lambda e, dt2=dt2: e.reciprocal(
                    out=dt2[0:2, :], in_=dt2[0:2, :]), "s_dv")
                prot = ps["pS"] if r2 == 0 else ps["pS2"]
                pr.wait("vector", "s_te", RT[i2])
                U2A[i2] = pr.inc("vector", lambda e, sl2=sl2, ct2=ct2, prot=prot:
                                 e.tensor_mul(
                    out=s32[:, sl2, :], in0=prot[:, :FT],
                    in1=sb["rtab"][:, ct2 + 1, :]), "s_dv")
                pr.wait("vector", "s_gp", U1[i2])
                pr.inc("vector", lambda e, sl2=sl2: e.tensor_add(
                    out=s32[:, sl2, :], in0=s32[:, sl2, :], in1=s32[:, sl2 + 1, :]),
                    "s_dv")

            def dve_fin(i2):
                qk2, c2 = chunks[i2]
                r2 = i2 % 2
                sl2 = 3 * r2
                pq = ps["pQ"] if r2 == 0 else ps["pV"]
                pr.wait("vector", "s_te", RB[i2])
                def mkst(e, qk2=qk2, c2=c2, sl2=sl2, pq=pq, cur=cur):
                    out = (sb["q_bf"][:, c2, :] if qk2 == 0
                           else sb["kbuf"][:, cur, c2, :])
                    return e.scalar_tensor_tensor(
                        out=out, in0=s32[:, sl2, :], scalar=1.0,
                        in1=pq[:, :FT], op0=OP.mult, op1=OP.mult)
                STT[i2] = pr.inc("vector", mkst, "s_dv")

        ii = len(chunks) - 1
        emit_qkv_te_tail(ii)
        act_tail(ii)
        dve_head(ii)
        te_rbc(ii)
        dve_fin(ii)
        pr.barrier(("tensor", "scalar", "vector", "gpsimd", "sync"))
        pr.wait("gpsimd", "s_te")
        gd = dma("gpsimd", "s_gd", sb["wv_sb"][:, :, :],
                 d[f"Wv{li}"].ap().rearrange("(k p) m -> p k m", p=128))
        pr.wait("tensor", "s_gd", gd)
        VE = {}
        for i in range(8):
            j, nh = i // 2, i % 2
            pp = ps["pA"] if i % 2 == 0 else ps["pB"]
            if i >= 2:
                pr.wait("tensor", "s_ac", VE[i - 2])
            def mkv(e, j=j, nh=nh, pp=pp):
                last = None
                for k in range(CC):
                    last = e.matmul(
                        out=pp[:, :FT],
                        lhsT=sb["zs_bf"][:, k, j * 128:(j + 1) * 128],
                        rhs=sb["wv_sb"][:, k, nh * 512:(nh + 1) * 512],
                        start=(k == 0), stop=(k == CC - 1))
                return last
            te = pr.inc("tensor", mkv, "s_te")
            pr.wait("scalar", "s_te", te)
            VE[i] = pr.inc("scalar", lambda e, j=j, nh=nh, pp=pp, cur=cur: e.copy(
                out=sb["vbuf"][:, cur, j, nh * 512:(nh + 1) * 512],
                in_=pp[:, :FT]), "s_ac")
        pr.barrier(("tensor", "scalar", "vector", "gpsimd", "sync"))

    def emit_attn_tile(li, ti, cur, zsrc, zdst):
        T0 = ti * FT
        s32 = sb["scratch32"]
        pr.wait("sync", "s_dv")
        pr.wait("sync", "s_ac")
        if ti == 0:
            l = dma("sync", "s_ld", sb["zt"][:, :, 1:513],
                    fm(zsrc)[:, :, 0:FT])
            pr.wait("vector", "s_ld", l)
            dv = pr.inc("vector", lambda e: e.tensor_copy(
                out=sb["zt"][:, :, 0:1], in_=sb["zstage"][:, :, 515:516]),
                "s_dv")
        else:
            l = dma("sync", "s_ld", sb["zt"][:, :, 0:513],
                    fm(zsrc)[:, :, T0 - 1:T0 + FT])
        dma("sync", "s_ld", sb["mskS_sb"][:, :, :],
            d["maskS"].ap().rearrange("(t j p) k -> p (t j) k", p=128, j=4)
            [:, ti * 4:(ti + 1) * 4, :])
        dma("sync", "s_ld", sb["mskP_sb"][:, :, :],
            d["maskP"].ap().rearrange("(t x p) k -> p (t x) k", p=128, x=12)
            [:, ti * 12:(ti + 1) * 12, :])
        pr.wait("sync", "s_ld")
        pr.barrier(("sync", "vector", "tensor", "scalar", "gpsimd"))
        emit_qkv(li, sb["zt"], 1, T0, cur, want_q=True)

        def kbv(kt_abs):
            if kt_abs < 0:
                return 1 - cur, kt_abs + 4
            return cur, kt_abs

        SM = {}; MX = {}
        for u in range(64):
            h, j = u // 4, u % 4
            ch, hh = h // 2, h % 2
            r = u % 2
            pp = ps["pS"] if r == 0 else ps["pS2"]
            if u >= 2:
                pr.wait("tensor", "s_dv", SM[u - 2])
            def mks(e, ch=ch, hh=hh, j=j, pp=pp):
                last = None
                for kt in range(3):
                    buf, jj = kbv(j - 2 + kt)
                    last = e.matmul(
                        out=pp[:, kt * 128:(kt + 1) * 128],
                        lhsT=sb["q_bf"][hh * 64:(hh + 1) * 64, ch,
                                        j * 128:(j + 1) * 128],
                        rhs=sb["kbuf"][hh * 64:(hh + 1) * 64, buf, ch,
                                       jj * 128:(jj + 1) * 128],
                        start=True, stop=True)
                return last
            te = pr.inc("tensor", mks, "s_te")
            pr.wait("vector", "s_te", te)
            pr.inc("vector", lambda e, j=j, pp=pp, r=r: e.tensor_add(
                out=s32[:, r, 0:384], in0=pp[:, 0:384],
                in1=sb["mskS_sb"][:, j, :]), "s_dv")
            SM[u] = pr.inc("vector", lambda e, u=u, r=r: e.reduce_max(
                out=sb["m_cols"][:, u:u + 1],
                in_=s32[:, r, 0:384], axis=AX.X), "s_dv")
        pr.wait("tensor", "s_dv", SM[63])
        te = pr.inc("tensor", lambda e: e.transpose(
            out=ps["pR"][0:64, 0:128], in_=sb["m_cols"][:, :],
            identity=sb["idf32_t"][:, :]), "s_te")
        pr.wait("scalar", "s_te", te)
        ac = pr.inc("scalar", lambda e: e.copy(
            out=sb["mrow_sb"][:, :], in_=ps["pR"][0:64, 0:128]), "s_ac")
        pr.wait("sync", "s_ac", ac)
        st = dma("sync", "s_st", d["mrow"].ap(), sb["mrow_sb"].ap())
        pr.wait("sync", "s_st", st)
        lb = dma("sync", "s_ld", sb["mrowr"][:, :], d["mrow"].ap())
        pr.wait("tensor", "s_ld", lb)
        STm = {}; EX = {}; MK = {}; PVm = {}; OE = {}; DNE = {}; SC = {}
        for u in range(64):
            h, j = u // 4, u % 4
            ch, hh = h // 2, h % 2
            r = u % 2
            pst = ps["pA"] if r == 0 else ps["pB"]
            pp_o = ps["pV"] if h % 2 == 0 else ps["pW"]
            if u >= 2:
                pr.wait("tensor", "s_ac", EX[u - 2])
            def mkst(e, ch=ch, hh=hh, j=j, u=u, pst=pst):
                last = None
                for kt in range(3):
                    buf, jj = kbv(j - 2 + kt)
                    e.matmul(
                        out=pst[:, kt * 128:(kt + 1) * 128],
                        lhsT=sb["kbuf"][hh * 64:(hh + 1) * 64, buf, ch,
                                        jj * 128:(jj + 1) * 128],
                        rhs=sb["q_bf"][hh * 64:(hh + 1) * 64, ch,
                                       j * 128:(j + 1) * 128],
                        start=True, stop=False)
                    last = e.matmul(
                        out=pst[:, kt * 128:(kt + 1) * 128],
                        lhsT=sb["negb_t"][:, :],
                        rhs=sb["mrowr"][:, u * 128:(u + 1) * 128],
                        start=False, stop=True)
                return last
            STm[u] = pr.inc("tensor", mkst, "s_te")
            if u >= 1:
                emit_pv(u - 1)
            pr.wait("scalar", "s_te", STm[u])
            if u >= 2:
                pr.wait("scalar", "s_dv", MK[u - 2])
            EX[u] = pr.inc("scalar", lambda e, r=r, pst=pst: e.activation(
                out=sb["pt_bf"][:, r, :], in_=pst[:, 0:384],
                func=AF.Exp, scale=QK_SCALE), "s_ac")
            pr.wait("vector", "s_ac", EX[u])
            MK[u] = pr.inc("vector", lambda e, j=j, r=r: e.tensor_mul(
                out=sb["pt_bf"][:, r, :], in0=sb["pt_bf"][:, r, :],
                in1=sb["mskP_sb"][:, j * 3:(j + 1) * 3, :].rearrange(
                    "p a b -> p (a b)")), "s_dv")

            def emit_pv(u2):
                h2, j2 = u2 // 4, u2 % 4
                r2 = u2 % 2
                pp2 = ps["pV"] if (u2 // 4) % 2 == 0 else ps["pW"]
                pr.wait("tensor", "s_dv", MK[u2])
                if j2 == 0 and u2 >= 8:
                    pr.wait("tensor", "s_ac", DNE[u2 // 4 - 2])
                def mkpv(e, h2=h2, j2=j2, r2=r2, pp2=pp2):
                    last = None
                    for kt in range(3):
                        buf, jj = kbv(j2 - 2 + kt)
                        e.matmul(
                            out=pp2[0:64, j2 * 128:(j2 + 1) * 128],
                            lhsT=sb["vbuf"][:, buf, jj, h2 * 64:(h2 + 1) * 64],
                            rhs=sb["pt_bf"][:, r2, kt * 128:(kt + 1) * 128],
                            start=(kt == 0), stop=(kt == 2))
                        last = e.matmul(
                            out=pp2[64:65, j2 * 128:(j2 + 1) * 128],
                            lhsT=sb["onescb_t"][:, :],
                            rhs=sb["pt_bf"][:, r2, kt * 128:(kt + 1) * 128],
                            start=(kt == 0), stop=(kt == 2))
                    return last
                PVm[u2] = pr.inc("tensor", mkpv, "s_te")
                if j2 == 3:
                    pr.wait("scalar", "s_te", PVm[u2])
                    if h2 >= 2:
                        pr.wait("scalar", "s_dv", SC[h2 // 2 - 1])
                    OE[h2] = pr.inc("scalar", lambda e, h2=h2, pp2=pp2: e.copy(
                        out=sb["o32"][(h2 % 2) * 64:(h2 % 2 + 1) * 64,
                                      h2 // 2, :],
                        in_=pp2[0:64, :FT]), "s_ac")
                    DNE[h2] = pr.inc("scalar", lambda e, h2=h2, pp2=pp2: e.copy(
                        out=sb["den"][0:1, (h2 % 2) * FT:(h2 % 2 + 1) * FT],
                        in_=pp2[64:65, :FT]), "s_ac")
                    if h2 % 2 == 1:
                        emit_chunk_scale(h2 // 2)

            def emit_chunk_scale(chk):
                pr.wait("vector", "s_ac", DNE[2 * chk + 1])
                dv = pr.inc("vector", lambda e: e.reciprocal(
                    out=sb["den"][0:1, :], in_=sb["den"][0:1, :]), "s_dv")
                pr.wait("tensor", "s_dv", dv)
                if chk >= 1:
                    pr.wait("tensor", "s_dv", SC[chk - 1])
                def mkbc(e):
                    e.matmul(out=ps["pQ"][:, :FT], lhsT=sb["blkA_t"][:, :],
                             rhs=sb["den"][0:1, 0:FT], start=True, stop=False)
                    return e.matmul(out=ps["pQ"][:, :FT],
                                    lhsT=sb["blkB_t"][:, :],
                                    rhs=sb["den"][0:1, FT:2 * FT],
                                    start=False, stop=True)
                te2 = pr.inc("tensor", mkbc, "s_te")
                pr.wait("vector", "s_te", te2)
                SC[chk] = pr.inc("vector", lambda e, chk=chk: e.scalar_tensor_tensor(
                    out=sb["zs_bf"][:, chk, :], in0=sb["o32"][:, chk, :],
                    scalar=1.0, in1=ps["pQ"][:, :FT],
                    op0=OP.mult, op1=OP.mult), "s_dv")

        emit_pv(63)
        pr.barrier(("tensor", "scalar", "vector", "gpsimd", "sync"))
        WE = {}
        for m in range(CC):
            buf = m % 2
            if m >= 2:
                pr.wait("gpsimd", "s_te", WE[m - 2][1])
            gd = dma("gpsimd", "s_gd", sb["wo_t"][:, buf, :, :],
                     d[f"Wo{li}"].ap().rearrange("(k p) m -> p k m", p=128)
                     [:, :, m * 128:(m + 1) * 128])
            pr.wait("tensor", "s_gd", gd)
            pp = ps["pA"] if m % 2 == 0 else ps["pB"]
            if m >= 2:
                pr.wait("tensor", "s_dv", WE[m - 2][0])
            def mko(e, buf=buf, pp=pp):
                last = None
                for k in range(CC):
                    last = e.matmul(out=pp[:, :FT],
                                    lhsT=sb["wo_t"][:, buf, k, :],
                                    rhs=sb["zs_bf"][:, k, :],
                                    start=(k == 0), stop=(k == CC - 1))
                return last
            te = pr.inc("tensor", mko, "s_te")
            pr.wait("vector", "s_te", te)
            dv = pr.inc("vector", lambda e, m=m, pp=pp: e.tensor_add(
                out=sb["o32"][:, m, :], in0=pp[:, :FT],
                in1=sb["zt"][:, m, 1:513]), "s_dv")
            WE[m] = (dv, te)
            pr.wait("sync", "s_dv", dv)
            dma("sync", "s_st", fm(zdst)[:, m, T0:T0 + FT], sb["o32"][:, m, :])
        pr.wait("sync", "s_st")
        pr.barrier()

    def emit_ffn_tile(li, ti, zsrc, zdst, last):
        T0 = ti * FT
        s32 = sb["scratch32"]
        rows = sb["rows"]
        Rr = lambda i: rows[0:1, i * TC: i * TC + FT]
        pr.wait("sync", "s_dv")
        pr.wait("sync", "s_ac")
        if ti == 0:
            l = dma("sync", "s_ld", sb["zt"][:, :, 1:513], fm(zsrc)[:, :, 0:FT])
            pr.wait("vector", "s_ld", l)
            pr.inc("vector", lambda e: e.tensor_copy(
                out=sb["zt"][:, :, 0:1], in_=sb["tok_sb"][:, :, 0:1]), "s_dv")
        else:
            l = dma("sync", "s_ld", sb["zt"][:, :, 0:513],
                    fm(zsrc)[:, :, T0 - 1:T0 + FT])
        if last:
            dma("sync", "s_ld", sb["o32"][:, :, :], fm(d["x1d"])[:, :, T0:T0 + FT])
        pr.wait("sync", "s_ld")
        pr.barrier(("sync", "vector", "tensor", "scalar", "gpsimd"))
        sq = None
        for c in range(CC):
            sq = pr.inc("scalar", lambda e, c=c: e.activation(
                out=s32[:, c, :], in_=shifted(sb["zt"], c, 1, FT),
                func=AF.Square), "s_ac")
        pr.wait("tensor", "s_ac", sq)
        def mk_stats(e):
            last = None
            for c in range(CC):
                last = e.matmul(out=ps["pS"][0:1, :FT], lhsT=sb["onesc_t"][:, :],
                                rhs=shifted(sb["zt"], c, 1, FT),
                                start=(c == 0), stop=(c == CC - 1))
            for c in range(CC):
                last = e.matmul(out=ps["pS2"][0:1, :FT], lhsT=sb["onesc_t"][:, :],
                                rhs=s32[:, c, :],
                                start=(c == 0), stop=(c == CC - 1))
            return last
        st = pr.inc("tensor", mk_stats, "s_te")
        pr.wait("vector", "s_te", st)
        pr.inc("vector", lambda e: e.tensor_scalar_mul(
            out=Rr(0), in0=ps["pS"][0:1, :FT], scalar1=1.0 / D), "s_dv")
        pr.inc("vector", lambda e: e.tensor_scalar_mul(
            out=Rr(1), in0=ps["pS2"][0:1, :FT], scalar1=1.0 / D), "s_dv")
        pr.inc("vector", lambda e: e.tensor_mul(out=Rr(2), in0=Rr(0), in1=Rr(0)), "s_dv")
        dv = pr.inc("vector", lambda e: e.tensor_sub(out=Rr(1), in0=Rr(1), in1=Rr(2)), "s_dv")
        pr.wait("scalar", "s_dv", dv)
        ac = pr.inc("scalar", lambda e: e.activation(
            out=Rr(2), in_=Rr(1), func=AF.Sqrt, bias=EPS), "s_ac")
        pr.wait("vector", "s_ac", ac)
        dv = pr.inc("vector", lambda e: e.reciprocal(out=Rr(3), in_=Rr(2)), "s_dv")
        pr.wait("tensor", "s_dv", dv)
        def mk_bc(e):
            e.matmul(out=ps["pS"][:, :FT], lhsT=sb["ones1_t"][:, :], rhs=Rr(0),
                     start=True, stop=True)
            return e.matmul(out=ps["pS2"][:, :FT], lhsT=sb["ones1_t"][:, :],
                            rhs=Rr(3), start=True, stop=True)
        bc = pr.inc("tensor", mk_bc, "s_te")
        pr.wait("vector", "s_te", bc)
        for c in range(CC):
            pr.inc("vector", lambda e, c=c: e.tensor_sub(
                out=s32[:, c, :], in0=shifted(sb["zt"], c, 1, FT),
                in1=ps["pS"][:, :FT]), "s_dv")
            pr.inc("vector", lambda e, c=c: e.tensor_mul(
                out=s32[:, c, :], in0=s32[:, c, :], in1=ps["pS2"][:, :FT]),
                "s_dv")
            dv = pr.inc("vector", lambda e, c=c: e.tensor_scalar(
                out=sb["znf_bf"][:, c, :], in0=s32[:, c, :],
                scalar1=sb["lng_t"][:, c:c + 1], scalar2=sb["lnbf_t"][:, c:c + 1],
                op0=OP.mult, op1=OP.add), "s_dv")
        pr.barrier(("tensor", "scalar", "vector", "gpsimd", "sync"))
        NPR = DFP // 128
        TE1 = {}; GE1 = {}; AG1 = {}
        pbanks = ((ps["pA"], ps["pB"]), (ps["pS"], ps["pS2"]))
        for prr in range(NPR):
            buf = prr % 2
            pa, pb = pbanks[buf]
            if prr >= 2:
                pr.wait("gpsimd", "s_te", TE1[prr - 2])
            gd = dma("gpsimd", "s_gd", sb["w1_t"][:, buf, :, :],
                     d[f"W1T{li}"].ap().rearrange("(k p) m -> p k m", p=128)
                     [:, :, prr * 256:(prr + 1) * 256])
            pr.wait("tensor", "s_gd", gd)
            if prr >= 2:
                pr.wait("tensor", "s_dv", AG1[prr - 2])
            def mk1(e, buf=buf, pa=pa, pb=pb):
                for k in range(CC):
                    e.matmul(out=pa[:, :FT],
                             lhsT=sb["w1_t"][:, buf, k, 0:128],
                             rhs=sb["znf_bf"][:, k, :],
                             start=(k == 0), stop=(k == CC - 1))
                last = None
                for k in range(CC):
                    last = e.matmul(out=pb[:, :FT],
                                    lhsT=sb["w1_t"][:, buf, k, 128:256],
                                    rhs=sb["znf_bf"][:, k, :],
                                    start=(k == 0), stop=(k == CC - 1))
                return last
            TE1[prr] = pr.inc("tensor", mk1, "s_te")
            pr.wait("scalar", "s_te", TE1[prr])
            if prr >= 2:
                pr.wait("scalar", "s_dv", AG1[prr - 2])
            GE1[prr] = pr.inc("scalar", lambda e, prr=prr, buf=buf, pb=pb:
                              e.activation(
                out=s32[:, buf, :], in_=pb[:, :FT], func=AF.Gelu,
                bias=sb["b1p_t"][:, 2 * prr + 1:2 * prr + 2]), "s_ac")
            pr.wait("vector", "s_ac", GE1[prr])
            AG1[prr] = pr.inc("vector", lambda e, prr=prr, buf=buf, pa=pa:
                              e.scalar_tensor_tensor(
                out=sb["ag_bf"][:, prr, :], in0=pa[:, :FT],
                scalar=sb["b1p_t"][:, 2 * prr:2 * prr + 1], in1=s32[:, buf, :],
                op0=OP.add, op1=OP.mult), "s_dv")
        pr.barrier(("tensor", "scalar", "vector", "gpsimd"))
        w2b = (sb["w1_t"], sb["w2_t"])
        TE2 = {}; DV2 = {}
        for m in range(CC):
            buf = m % 2
            wt = w2b[buf]
            if m >= 2:
                pr.wait("gpsimd", "s_te", TE2[m - 2])
            spec = "p a b c -> p (a b c)" if buf == 0 else "p a b -> p (a b)"
            wflat = wt.ap().rearrange(spec)[:, 0:NPR * 128] \
                .rearrange("p (a b) -> p a b", b=128)
            gd = dma("gpsimd", "s_gd", wflat,
                     d[f"W2T{li}"].ap().rearrange("(k p) m -> p k m", p=128)
                     [:, :, m * 128:(m + 1) * 128])
            pr.wait("tensor", "s_gd", gd)
            pp = ps["pA"] if m % 2 == 0 else ps["pB"]
            if m >= 2:
                pr.wait("tensor", "s_dv", DV2[m - 2])
            def mk2(e, wv=wflat, pp=pp):
                last = None
                for k in range(NPR):
                    last = e.matmul(
                        out=pp[:, :FT],
                        lhsT=wv[:, k, :],
                        rhs=sb["ag_bf"][:, k, :],
                        start=(k == 0), stop=(k == NPR - 1))
                return last
            TE2[m] = pr.inc("tensor", mk2, "s_te")
            pr.wait("vector", "s_te", TE2[m])
            dv = pr.inc("vector", lambda e, m=m, pp=pp: e.scalar_tensor_tensor(
                out=s32[:, m, :], in0=pp[:, :FT],
                scalar=sb["b2_t"][:, m:m + 1], in1=sb["zt"][:, m, 1:513],
                op0=OP.add, op1=OP.add), "s_dv")
            if last:
                dv = pr.inc("vector", lambda e, m=m: e.tensor_add(
                    out=s32[:, m, :], in0=s32[:, m, :], in1=sb["o32"][:, m, :]),
                    "s_dv")
            DV2[m] = dv
            pr.wait("sync", "s_dv", dv)
            dst = d["out"] if last else zdst
            dma("sync", "s_st", fm(dst)[:, m, T0:T0 + FT], s32[:, m, :])
        pr.wait("sync", "s_st")
        pr.barrier()

    def emit_ffn_all():
        for li in range(STAGE_LAYERS):
            zsrc, zdst = d["za"], d["zb"]
            dma("sync", "s_ld", sb["lng_t"][:, :], d[f"lng{li}"][:, :])
            dma("sync", "s_ld", sb["lnbf_t"][:, :], d[f"lnbf{li}"][:, :])
            dma("sync", "s_ld", sb["b1p_t"][:, :], d[f"b1p{li}"][:, :])
            dma("sync", "s_ld", sb["b2_t"][:, :], d[f"b2{li}"][:, :])
            pr.wait("sync", "s_ld")
            pr.barrier()
            emit_halo_exchange(zsrc, li)
            for pos, ti in enumerate(TLORD):
                cur = pos % 2
                if ti == 2:
                    pr.wait("sync", "s_dv")
                    pr.wait("sync", "s_ac")
                    l = dma("sync", "s_ld", sb["zt"][:, :, 0:513],
                            fm(zsrc)[:, :, FT - 1:2 * FT])
                    pr.wait("sync", "s_ld", l)
                    pr.barrier(("sync", "vector", "tensor", "scalar", "gpsimd"))
                    emit_qkv(li, sb["zt"], 1, FT, 1 - cur, want_q=False)
                elif ti == 0:
                    emit_qkv(li, sb["zstage"], 4, -FT, 1 - cur, want_q=False)
                emit_attn_tile(li, ti, cur, zsrc, zdst)
            emit_tok_exchange(zdst)
            last = (li == DEPTH - 1) and (STAGE_LAYERS == DEPTH)
            for ti in [1, 2, 3, 4, 5, 6, 7, 0]:
                emit_ffn_tile(li, ti, zdst, zsrc, last)

    emit_load_consts()
    emit_load_phase_weights(1)
    emit_mamba_phase(1)
    if 2 in DEV_PHASES:
        emit_carry_exchange()
        emit_load_phase_weights(2)
        emit_mamba_phase(2)
        pr.barrier()
        emit_combine()

    if STAGE_LAYERS < 0:
        for c in range(NCH):
            pr.wait("sync", "s_st")
            l = dma("sync", "s_ld", sb["xin"][:, :, :],
                    fm(d[DEV_OUT])[:, :, c * TC:(c + 1) * TC])
            pr.wait("sync", "s_ld", l)
            dma("sync", "s_st", fm(d["out"])[:, :, c * TC:(c + 1) * TC],
                sb["xin"][:, :, :])
        pr.wait("sync", "s_st")
    else:
        emit_ffn_all()
        if STAGE_LAYERS < DEPTH:
            for c in range(NCH):
                pr.wait("sync", "s_st")
                l = dma("sync", "s_ld", sb["xin"][:, :, :],
                        fm(d[DEV_OUT if DEV_OUT != "x1d" else "za"])[:, :, c * TC:(c + 1) * TC])
                pr.wait("sync", "s_ld", l)
                dma("sync", "s_st", fm(d["out"])[:, :, c * TC:(c + 1) * TC],
                    sb["xin"][:, :, :])
        pr.wait("sync", "s_st")
    pr.barrier()

    with nc.Block() as block:
        @block.sync
        def _(e):
            for fn in pr.q["sync"]:
                fn(e)

        @block.tensor
        def _(e):
            for fn in pr.q["tensor"]:
                fn(e)

        @block.vector
        def _(e):
            for fn in pr.q["vector"]:
                fn(e)

        @block.scalar
        def _(e):
            for fn in pr.q["scalar"]:
                fn(e)

        @block.gpsimd
        def _(e):
            for fn in pr.q["gpsimd"]:
                fn(e)

    build_program._sb = sb
    build_program._nc = nc
    ctx.close()
    return nc


def _conv_diag(Wc):
    out = np.zeros((128, DC * 4 * 128), np.float32)
    for dc in range(DC):
        for k in range(4):
            blk = np.diag(Wc[dc * 128:(dc + 1) * 128, k])
            out[:, (dc * 4 + k) * 128:(dc * 4 + k + 1) * 128] = blk
    return out


def _p128(v):
    return np.ascontiguousarray(v.reshape(DC, 128).T.astype(np.float32))


def _host_core_inputs(inputs, b, h):
    x = np.asarray(inputs["L0_attn_x"][b], np.float32)
    xa = x[0:T]
    xb = x[T:2 * T]
    im = {}
    z508 = np.zeros((508, D), np.float32)
    if h == 0:
        own = xa
        xs1 = np.concatenate([z508, np.zeros((4, D), np.float32), own], 0)
        ext = np.concatenate([own, xb[0:4]], 0)
        xs2 = np.concatenate([z508, ext[::-1]], 0)
        w1, w2 = "f_", "b_"
        m_id1, m_id2 = 1.0, 0.0
        m_lo, m_hi = 0.0, 1.0
    else:
        own = xb
        xs1 = np.concatenate([z508, np.zeros((4, D), np.float32), own[::-1]], 0)
        ext = np.concatenate([xa[T - 4:T], own], 0)
        xs2 = np.concatenate([z508, ext], 0)
        w1, w2 = "b_", "f_"
        m_id1, m_id2 = 0.0, 1.0
        m_lo, m_hi = 1.0, 0.0
    im["xs1"] = np.ascontiguousarray(xs1.T)
    im["xs2"] = np.ascontiguousarray(xs2.T)
    im["xnat"] = np.ascontiguousarray(own.T)
    for p, pref in ((1, w1), (2, w2)):
        g = lambda nm: np.asarray(inputs[f"L0_attn_{pref}{nm}"], np.float32)
        im[f"WinT{p}"] = np.ascontiguousarray(g("W_in").T)
        im[f"Wconvd{p}"] = _conv_diag(g("W_conv"))
        im[f"WxT{p}"] = np.ascontiguousarray(g("W_x").T)
        im[f"WdtT{p}"] = np.ascontiguousarray(g("W_dt").T)
        im[f"WoutT{p}"] = np.ascontiguousarray(g("W_out").T)
        im[f"Alog{p}"] = np.ascontiguousarray(
            g("A_log").reshape(DC, 128, NST).transpose(1, 0, 2).reshape(128, DC * NST))
        im[f"bconv{p}"] = _p128(g("b_conv"))
        im[f"bdt{p}"] = _p128(g("b_dt"))
        im[f"Dp{p}"] = _p128(g("D"))
    im["ln_w"] = np.ascontiguousarray(
        np.asarray(inputs["L0_attn_ln_w"], np.float32).reshape(CC, 128).T)
    im["ln_b"] = np.ascontiguousarray(
        np.asarray(inputs["L0_attn_ln_b"], np.float32).reshape(CC, 128).T)
    im["ones1"] = np.ones((1, 128), np.float32)
    im["ones_c"] = np.ones((128, 1), np.float32)
    cv = np.zeros((128, 16), np.float32)
    cv[:, :8] = np.asarray([EPS, 1.0, -1.0, QK_SCALE, 0.5, 1.0 / D, 0.0, 2.0])
    im["cvals"] = cv
    im["onesb"] = np.ones((1, 128), ml_dtypes.bfloat16)
    for nm, v in (("m_id1", m_id1), ("m_rev1", 1.0 - m_id1),
                  ("m_id2", m_id2), ("m_rev2", 1.0 - m_id2),
                  ("m_lo", m_lo), ("m_hi", m_hi)):
        im[nm] = np.full((128, 1), v, np.float32)
    if STAGE_LAYERS >= 0:
        im.update(_ffn_host_inputs(inputs, h))
    return im




def _ffn_host_inputs(inputs, h):
    im = {}
    asc = np.ascontiguousarray
    for li in range(DEPTH):
        Wqkv = np.asarray(inputs["L0_ffn_Wqkv"][li], np.float32)
        im[f"WqkT{li}"] = asc(Wqkv[:, 0:2 * D])
        im[f"Wv{li}"] = asc(Wqkv[:, 2 * D:3 * D])
        im[f"Wo{li}"] = asc(np.asarray(inputs["L0_ffn_Wo"][li], np.float32))
        im[f"lng{li}"] = asc(np.asarray(inputs["L0_ffn_ln_g"][li],
                                        np.float32).reshape(CC, 128).T)
        im[f"lnbf{li}"] = asc(np.asarray(inputs["L0_ffn_ln_b"][li],
                                         np.float32).reshape(CC, 128).T)
        W1 = np.asarray(inputs["L0_ffn_W1"][li], np.float32)
        a = np.zeros((D, DFP), np.float32); a[:, :DFI] = W1[:, :DFI]
        g = np.zeros((D, DFP), np.float32); g[:, :DFI] = W1[:, DFI:2 * DFI]
        NPR = DFP // 128
        w1p = np.zeros((D, 2 * DFP), np.float32)
        for i in range(NPR):
            w1p[:, (2 * i) * 128:(2 * i + 1) * 128] = a[:, i * 128:(i + 1) * 128]
            w1p[:, (2 * i + 1) * 128:(2 * i + 2) * 128] = g[:, i * 128:(i + 1) * 128]
        im[f"W1T{li}"] = asc(w1p)
        b1 = np.asarray(inputs["L0_ffn_b1"][li], np.float32)
        b1a = np.zeros(DFP, np.float32); b1a[:DFI] = b1[:DFI]
        b1g = np.zeros(DFP, np.float32); b1g[:DFI] = b1[DFI:]
        b1p = np.zeros((128, 2 * NPR), np.float32)
        for i in range(NPR):
            b1p[:, 2 * i] = b1a[i * 128:(i + 1) * 128]
            b1p[:, 2 * i + 1] = b1g[i * 128:(i + 1) * 128]
        im[f"b1p{li}"] = b1p
        W2 = np.asarray(inputs["L0_ffn_W2"][li], np.float32)
        w2p = np.zeros((DFP, D), np.float32); w2p[:DFI] = W2
        im[f"W2T{li}"] = asc(w2p)
        im[f"b2{li}"] = asc(np.asarray(inputs["L0_ffn_b2"][li],
                                       np.float32).reshape(CC, 128).T)
    qs = np.asarray(inputs["L0_ffn_q_scale"], np.float32)
    ks = np.asarray(inputs["L0_ffn_k_scale"], np.float32)
    dd = np.arange(128) % 64
    im["qsp"] = asc(qs[:, dd].T)
    im["ksp"] = asc(ks[:, dd].T)
    G0 = h * T
    t = np.arange(G0 - 512, G0 + T, dtype=np.float64)
    half = np.arange(0, DH, 2, dtype=np.float64)
    inv_freq = 1.0 / 10000.0 ** (half / DH)
    fr = t[:, None] * inv_freq[None, :]
    freqs = np.concatenate([fr, fr], -1)
    base = (half + 0.4 * DH) / (1.4 * DH)
    power = (t - TG // 2) / (WIN // 2)
    scale = base[None, :] ** power[:, None]
    scale = np.concatenate([scale, scale], -1)
    cosv = np.cos(freqs); sinv = np.sin(freqs)
    sgn = np.where(dd < 32, -1.0, 1.0)
    im["rcq"] = asc((cosv[:, dd] * scale[:, dd]).T.astype(np.float32))
    im["rsq"] = asc((sinv[:, dd] * scale[:, dd] * sgn[None, :]).T.astype(np.float32))
    im["rck"] = asc((cosv[:, dd] / scale[:, dd]).T.astype(np.float32))
    im["rsk"] = asc((sinv[:, dd] / scale[:, dd] * sgn[None, :]).T.astype(np.float32))
    NEGF = np.float32(NEG)
    mS = np.zeros((NFT, 4, 128, 384), np.float32)
    mP = np.zeros((NFT, 4, 3, 128, 128), np.float32)
    for ti in range(NFT):
        for j in range(4):
            itl = G0 + ti * FT + j * 128 + np.arange(128)
            jbl = G0 + ti * FT + j * 128 - 256 + np.arange(384)
            keep = ((jbl[None, :] <= itl[:, None])
                    & (itl[:, None] <= jbl[None, :] + WIN)
                    & (jbl[None, :] >= 0))
            mS[ti, j] = np.where(keep, 0.0, NEGF)
            for kt in range(3):
                mP[ti, j, kt] = keep[:, kt * 128:(kt + 1) * 128].T
    im["maskS"] = asc(mS.reshape(NFT * 4 * 128, 384))
    im["maskP"] = asc(mP.reshape(NFT * 4 * 3 * 128, 128).astype(ml_dtypes.bfloat16))
    blk2 = np.zeros((128, 2), np.float32)
    blk2[0:64, 0] = 1.0; blk2[64:128, 1] = 1.0
    im["blk2"] = blk2
    im["blk2b"] = asc(blk2.T)
    blkA = np.zeros((1, 128), np.float32); blkA[0, 0:64] = 1.0
    blkB = np.zeros((1, 128), np.float32); blkB[0, 64:128] = 1.0
    im["blkA"] = blkA; im["blkB"] = blkB
    im["negb"] = np.full((1, 128), -1.0, ml_dtypes.bfloat16)
    im["onescb"] = np.ones((128, 1), ml_dtypes.bfloat16)
    im["idf32"] = np.eye(128, dtype=np.float32)
    rotm = np.zeros((128, 128), np.float32)
    for m in range(128):
        srcp = m + 32 if (m % 64) < 32 else m - 32
        rotm[srcp, m] = 1.0
    im["rotm"] = rotm
    return im

_NC_CACHE = {}


def kernel(**inputs):
    key = (STAGE_LAYERS, DEV_OUT, DEV_DEBUG, DEV_NCH, tuple(DEV_PHASES))
    if key not in _NC_CACHE:
        _NC_CACHE[key] = build_program()
    nc = _NC_CACHE[key]
    in_maps = []
    for b in range(B):
        for h in range(2):
            in_maps.append(_host_core_inputs(inputs, b, h))
    res = run_bass_kernel_spmd(nc, in_maps, core_ids=list(range(8)))
    out = np.zeros((B, TG, D), np.float32)
    for ci, (b, h) in enumerate((b, h) for b in range(B) for h in range(2)):
        o = np.asarray(res.results[ci]["out"])
        out[b, h * T:(h + 1) * T] = o.T
    if DEV_DEBUG:
        kernel._dbg = res.results
    return out
